# version 11
# speedup vs baseline: 1.4913x; 1.0699x over previous
import numpy as np
import ml_dtypes
from contextlib import ExitStack
import concourse.bass as bass
import concourse.mybir as mybir
from concourse.bass_utils import run_bass_kernel_spmd

F32 = mybir.dt.float32
BF16 = mybir.dt.bfloat16
AF = mybir.ActivationFunctionType
ALU = mybir.AluOpType

D = 1024
NCH = D // 128
NHEAD = 16
HS = 64
KVL = 512
QL = 512
LN_EPS = 1e-5


class Buf:
    __slots__ = ("w", "r", "name")
    def __init__(self, name=""):
        self.w = None
        self.r = set()
        self.name = name

class K:
    SECTIONS = {"pe": "tensor", "act": "scalar", "dve": "vector",
                "pool": "gpsimd", "sp": "sync"}
    DMAQ = {"pool": "qpool", "sp": "qsp", "act": "qact"}
    NQROT = 4

    def __init__(self, nc):
        self.nc = nc
        self.ops = []
        self.final_barrier_stream = None
        self.semnames = ["pe", "act", "dve", "pool", "sp"]
        for q in self.DMAQ.values():
            for i in range(self.NQROT):
                self.semnames.append(f"{q}{i}")

    def op(self, stream, fn, reads=(), writes=(), dma=False, deps=(), dur=500):
        dl = set()
        for t in deps:
            if t is not None:
                dl.add(t)
        for b in reads:
            if b.w is not None:
                dl.add(b.w)
        for b in writes:
            if b.w is not None:
                dl.add(b.w)
            dl.update(b.r)
        oid = len(self.ops)
        self.ops.append([stream, fn, dma, dl, dur])
        for b in reads:
            b.r.add(oid)
        for b in writes:
            b.w = oid
            b.r = set()
        return oid

    def barrier(self, stream, deps):
        dl = set()
        for t in deps:
            if isinstance(t, Buf):
                if t.w is not None:
                    dl.add(t.w)
            elif t is not None:
                dl.add(t)
        self.ops.append([stream, None, False, dl, 10])
        self.final_barrier_stream = stream
        return len(self.ops) - 1

    def _schedule(self):
        import heapq
        n = len(self.ops)
        pend = {s: [] for s in self.SECTIONS}
        for oid, rec in enumerate(self.ops):
            pend[rec[0]].append(oid)
        pos = {s: 0 for s in self.SECTIONS}
        done_t = [None] * n
        scheduled = [False] * n
        engine_t = {s: 0.0 for s in self.SECTIONS}
        order = {s: [] for s in self.SECTIONS}
        WINDOW = 400
        evq = [(0.0, s) for s in self.SECTIONS]
        heapq.heapify(evq)
        n_sched = 0
        guard = 0
        while n_sched < n:
            guard += 1
            if guard > 40 * n:
                raise RuntimeError("scheduler stuck")
            now, s = heapq.heappop(evq)
            lst = pend[s]
            while pos[s] < len(lst) and scheduled[lst[pos[s]]]:
                pos[s] += 1
            if pos[s] >= len(lst):
                continue
            best = None
            best_ready = None
            next_wake = None
            for idx in range(pos[s], min(pos[s] + WINDOW, len(lst))):
                oid = lst[idx]
                if scheduled[oid]:
                    continue
                rec = self.ops[oid]
                ready = 0.0
                ok = True
                for dep in rec[3]:
                    dt_ = done_t[dep]
                    if dt_ is None:
                        ok = False
                        break
                    if dt_ > ready:
                        ready = dt_
                if not ok:
                    continue
                if ready <= now + 1.0:
                    best = oid
                    break
                if next_wake is None or ready < next_wake:
                    next_wake = ready
                    best_ready = oid
            if best is None:
                if next_wake is None:
                    oid = lst[pos[s]]
                    rec = self.ops[oid]
                    wake = None
                    for dep in rec[3]:
                        dt_ = done_t[dep]
                        if dt_ is None:
                            wake = None
                            break
                        wake = dt_ if wake is None else max(wake, dt_)
                    if wake is None:
                        heapq.heappush(evq, (now + 100.0, s))
                    else:
                        heapq.heappush(evq, (max(wake, now + 1.0), s))
                    continue
                best = best_ready
                now = max(now, next_wake)
            rec = self.ops[best]
            start = max(now, engine_t[s])
            for dep in rec[3]:
                start = max(start, done_t[dep])
            if rec[2]:
                engine_t[s] = start + 50.0
                done_t[best] = start + rec[4]
            else:
                engine_t[s] = start + rec[4]
                done_t[best] = engine_t[s]
            scheduled[best] = True
            order[s].append(best)
            n_sched += 1
            heapq.heappush(evq, (engine_t[s], s))
        return order

    def playback(self, block, sems):
        order = self._schedule()
        for s in self.SECTIONS:
            bar = [oid for oid in order[s] if self.ops[oid][1] is None]
            if bar:
                order[s] = [oid for oid in order[s]
                            if self.ops[oid][1] is not None] + bar
        cnt = {s: 0 for s in self.semnames}
        qrot = {q: 0 for q in self.DMAQ.values()}
        qslot_cnt = {}
        tick = [None] * len(self.ops)
        extra_wait = [None] * len(self.ops)
        for s in self.SECTIONS:
            for oid in order[s]:
                rec = self.ops[oid]
                if rec[1] is None:
                    continue
                if rec[2]:
                    q = self.DMAQ[s]
                    j = qrot[q]
                    qrot[q] = (j + 1) % self.NQROT
                    sem = f"{q}{j}"
                    if cnt[sem]:
                        extra_wait[oid] = (sem, cnt[sem])
                    cnt[sem] += 16
                    tick[oid] = (sem, cnt[sem])
                else:
                    cnt[s] += 1
                    tick[oid] = (s, cnt[s])
        final_qwaits = [(f"{q}{i}", cnt[f"{q}{i}"]) for q in self.DMAQ.values()
                        for i in range(self.NQROT) if cnt[f"{q}{i}"]]
        known = {}
        for s, section in self.SECTIONS.items():
            ops = order[s]
            if not ops:
                continue
            prog = []
            own_done = set()
            for oid in ops:
                rec = self.ops[oid]
                waits = []
                dl = list(rec[3])
                if extra_wait[oid] is not None:
                    waits.append(extra_wait[oid])
                for dep in dl:
                    tk = tick[dep]
                    if tk is None:
                        continue
                    sem, val = tk
                    if s == "pe" and sem == "pe":
                        continue
                    kk = (s, sem)
                    if known.get(kk, 0) < val:
                        waits.append((sem, val))
                        known[kk] = val
                if rec[1] is None and self.final_barrier_stream == s:
                    waits.extend(w for w in final_qwaits
                                 if known.get((s, w[0]), 0) < w[1])
                prog.append((rec[1], waits, tick[oid]))
            def body(eng, prog=prog):
                for fn, waits, tk in prog:
                    for (sem, val) in waits:
                        eng.wait_ge(sems[sem], val)
                    if fn is not None:
                        fn(eng).then_inc(sems[tk[0]], 16 if tk[0][0] == "q"
                                         else 1)
            getattr(block, section)(body)


def _fd(ap):
    n = 1
    for d in ap.shape[1:]:
        n *= d
    return n

def mm(k, out, lhsT, rhs, start, stop, reads, writes, tp=None):
    n = rhs.shape[-1]
    return k.op("pe", lambda e, o=out, l=lhsT, r=rhs, a=start, b=stop, tp=tp:
                e.matmul(o, l, r, start=a, stop=b, tile_position=tp),
                reads=reads, writes=writes, dur=140 + n * 0.45)

def tr(k, out, in_, ident, kdim, reads, writes):
    return k.op("pe", lambda e, o=out, i=in_, id_=ident, kd=kdim:
                e.transpose(o, i, id_[0:kd, 0:kd]), reads=reads, writes=writes,
                dur=190)

def act(k, out, in_, func, reads, writes, bias=None, scale=None, accum=None):
    kw = {}
    if bias is not None:
        kw["bias"] = bias
    if scale is not None:
        kw["scale"] = scale
    if accum is not None:
        kw["accum_out"] = accum
    return k.op("act", lambda e, o=out, i=in_, f=func, kw=kw:
                e.activation(o, i, f, **kw), reads=reads, writes=writes,
                dur=330 + _fd(out) * 0.85)

def acopy(k, out, in_, reads, writes):
    return k.op("act", lambda e, o=out, i=in_: e.copy(o, i),
                reads=reads, writes=writes, dur=300 + _fd(out) * 0.85)

def vcopy(k, out, in_, reads, writes):
    return k.op("dve", lambda e, o=out, i=in_: e.tensor_copy(o, i),
                reads=reads, writes=writes, dur=250 + _fd(out) * 0.6)

def tt(k, out, a, b, op, reads, writes):
    return k.op("dve", lambda e, o=out, x=a, y=b, op=op: e.tensor_tensor(o, x, y, op),
                reads=reads, writes=writes, dur=280 + _fd(out) * 1.1)

def ts(k, out, in_, s1, s2, op0, op1, reads, writes):
    if op1 is None:
        return k.op("dve", lambda e, o=out, i=in_, s1=s1, op0=op0:
                    e.tensor_scalar(o, i, s1, None, op0), reads=reads,
                    writes=writes, dur=260 + _fd(out) * 0.6)
    return k.op("dve", lambda e, o=out, i=in_, s1=s1, s2=s2, op0=op0, op1=op1:
                e.tensor_scalar(o, i, s1, s2, op0, op1), reads=reads,
                writes=writes, dur=260 + _fd(out) * 0.6)

def stt(k, out, in0, scalar, in1, op0, op1, reads, writes):
    return k.op("dve", lambda e, o=out, i0=in0, s=scalar, i1=in1, op0=op0, op1=op1:
                e.scalar_tensor_tensor(o, i0, s, i1, op0, op1),
                reads=reads, writes=writes, dur=280 + _fd(out) * 1.1)

def vrecip(k, out, in_, reads, writes):
    return k.op("dve", lambda e, o=out, i=in_: e.reciprocal(o, i),
                reads=reads, writes=writes, dur=260 + _fd(out) * 1.1)

def vmemset(k, ap_, val, writes):
    return k.op("dve", lambda e, a=ap_, v=val: e.memset(a, v), writes=writes,
                dur=260 + _fd(ap_) * 0.3)

def bnstats(k, out, in_, reads, writes):
    return k.op("dve", lambda e, o=out, i=in_: e.bn_stats(o, i),
                reads=reads, writes=writes, dur=260 + _fd(in_) * 1.1)

def bnaggr(k, out, in_, reads, writes):
    return k.op("dve", lambda e, o=out, i=in_: e.bn_aggr(o, i),
                reads=reads, writes=writes, dur=260)

def ptt(k, out, a, b, op, reads, writes):
    return k.op("pool", lambda e, o=out, x=a, y=b, op=op:
                e.tensor_tensor(o, x, y, op),
                reads=reads, writes=writes, dur=300 + _fd(out) * 2.2)

def dma(k, stream, out, in_, reads, writes):
    try:
        nb = out.nbytes() if callable(out.nbytes) else out.nbytes
    except Exception:
        nb = 1 << 16
    return k.op(stream, lambda e, o=out, i=in_: e.dma_start(out=o, in_=i),
                reads=reads, writes=writes, dma=True, dur=1800 + nb * 0.0028)

class Arena:
    def __init__(self, nc, stack, name, words_per_partition):
        self.t = stack.enter_context(
            nc.sbuf_tensor(name, [128, words_per_partition], F32))
        self.free = [[0, words_per_partition, set()]]
        self.live = {}

    def alloc(self, name, shape, dtype):
        assert shape[0] <= 128
        assert name not in self.live, name
        elems = int(np.prod(shape[1:]))
        words = (elems + 1) // 2 if dtype == BF16 else elems
        words = (words + 7) & ~7
        top = words <= 512
        order = reversed(list(enumerate(self.free))) if top else enumerate(self.free)
        for i, (off, sz, pend) in order:
            if sz >= words:
                if sz == words:
                    self.free.pop(i)
                elif top:
                    self.free[i] = [off, sz - words, set(pend)]
                    off = off + sz - words
                else:
                    self.free[i] = [off + words, sz - words, set(pend)]
                ap = self.t[0:shape[0], off:off + words]
                if dtype == BF16:
                    ap = ap.bitcast(BF16)[:, 0:elems]
                else:
                    ap = ap[:, 0:elems]
                if len(shape) > 2:
                    pat = "p (" + " ".join(f"a{i}" for i in range(len(shape) - 1)) + ") -> p " + \
                          " ".join(f"a{i}" for i in range(len(shape) - 1))
                    kw = {f"a{i}": shape[1 + i] for i in range(len(shape) - 1)}
                    ap = ap.rearrange(pat, **kw)
                buf = Buf(name)
                buf.r = set(pend)
                self.live[name] = (off, words, buf)
                return ap, buf
        raise MemoryError(f"arena full allocating {name} ({words} words); "
                          f"free={[(o, s) for o, s, _ in self.free]}")

    def release(self, name, extra_bufs=()):
        off, words, buf = self.live.pop(name)
        pend = set()
        for b in (buf,) + tuple(extra_bufs):
            pend.update(b.r)
            if b.w is not None:
                pend.add(b.w)
        self.free.append([off, words, pend])
        self.free.sort(key=lambda x: x[0])
        merged = []
        for o, s, p in self.free:
            if merged and merged[-1][0] + merged[-1][1] == o:
                merged[-1][2] = merged[-1][2] | p
                merged[-1][1] += s
            else:
                merged.append([o, s, set(p)])
        self.free = merged


class Ctx:
    pass

def make_psum(nc, stack):
    banks = []
    for i in range(6):
        t = stack.enter_context(nc.psum_tensor(f"pb{i}", [128, 512], F32))
        banks.append((t, Buf(f"pb{i}")))
    tb = stack.enter_context(nc.psum_tensor("pbT", [128, 1024], BF16))
    tb2 = stack.enter_context(nc.psum_tensor("pbT2", [128, 1024], BF16))
    return banks, (tb, Buf("pbT")), (tb2, Buf("pbT2"))

def constcol(c, val):
    key = float(val)
    if key not in c._cc:
        ap, b = c.ar.alloc(f"cc_{len(c._cc)}", [128, 1], F32)
        vmemset(c.k, ap, key, [b])
        c._cc[key] = (ap, b)
    return c._cc[key]

def emit_constants(c):
    k, nc = c.k, c.nc
    c._cc = {}
    c.dif, c.dif_b = c.ar.alloc("dif", [128, 128], F32)
    k.op("pool", lambda e, a=c.dif: e.iota(a, pattern=[[1, 128]], base=0,
                                           channel_multiplier=-1,
                                           allow_small_or_imprecise_dtypes=True),
         writes=[c.dif_b])
    c.ident, c.ident_b = c.ar.alloc("ident", [128, 128], BF16)
    c.identf, c.identf_b = c.ar.alloc("identf", [128, 128], F32)
    c.mtri, c.mtri_b = c.ar.alloc("mtri", [128, 128], BF16)
    ts(c.k, c.ident, c.dif, 0.0, None, ALU.is_equal, None, [c.dif_b], [c.ident_b])
    ts(c.k, c.identf, c.dif, 0.0, None, ALU.is_equal, None, [c.dif_b], [c.identf_b])
    ts(c.k, c.mtri, c.dif, 0.0, None, ALU.is_ge, None, [c.dif_b], [c.mtri_b])
    c.ones_row, c.ones_row_b = c.ar.alloc("ones_row", [1, 128], BF16)
    c.ones_rowf, c.ones_rowf_b = c.ar.alloc("ones_rowf", [1, 128], F32)
    c.ones_col, c.ones_col_b = c.ar.alloc("ones_col", [128, 1], BF16)
    vmemset(k, c.ones_row, 1.0, [c.ones_row_b])
    vmemset(k, c.ones_rowf, 1.0, [c.ones_rowf_b])
    vmemset(k, c.ones_col, 1.0, [c.ones_col_b])
    c.striu, c.striu_b = c.ar.alloc("striu", [128, 128], BF16)
    ts(c.k, c.striu, c.dif, 0.0, None, ALU.is_gt, None, [c.dif_b], [c.striu_b])
    c.io512, c.io512_b = c.ar.alloc("io512", [128, 512], F32)
    k.op("pool", lambda e, a=c.io512: e.iota(a, pattern=[[1, 512]], base=0,
                                             channel_multiplier=0,
                                             allow_small_or_imprecise_dtypes=True),
         writes=[c.io512_b])

_DVE_BANKS = {"pb1", "pb3", "pb5", "pbT2"}

def evac(c, dst_ap, dst_buf, src_ap, src_buf, scale=None, bias=None,
         bias_buf=None):
    use_dve = src_buf.name in _DVE_BANKS
    if scale is None and bias is None:
        if use_dve:
            return vcopy(c.k, dst_ap, src_ap, [src_buf], [dst_buf])
        return acopy(c.k, dst_ap, src_ap, [src_buf], [dst_buf])
    reads = [src_buf] + ([bias_buf] if bias_buf is not None else [])
    if use_dve and scale is None:
        return ts(c.k, dst_ap, src_ap, 1.0, bias, ALU.mult, ALU.add,
                  reads, [dst_buf])
    return act(c.k, dst_ap, src_ap, AF.Identity, reads, [dst_buf],
               bias=bias, scale=scale)


def emit_ln_pipe(c, xv, xbufs, nt, eps, nm, fm=None, fm_b=None, tm=None,
                 tm_b=None, keep=False, t0=0):
    k = c.k
    sb6 = [c.ar.alloc(nm + f"_6{i}", [128, 12], F32) for i in range(2)]
    agb = [c.ar.alloc(nm + f"_ag{i}", [128, 2], F32) for i in range(2)]
    rstd, rstd_b = c.ar.alloc(nm + "_r", [128, t0 + nt], F32)
    mm_, mm_b = c.ar.alloc(nm + "_m", [128, t0 + nt], F32)
    eps_ap, eps_b = constcol(c, eps)
    stg, stg_b = (None, None)
    if fm is not None and tm is None:
        stgs = [c.ar.alloc(nm + f"_stg{i}", [128, 1024], BF16) for i in range(2)]
    for i in range(nt):
        t = t0 + i
        xt = xv[:, t, :]
        xb = xbufs[t]
        st6, st6_b = sb6[i % 2]
        ag, ag_b = agb[i % 2]
        for h in range(2):
            bnstats(k, st6[:, 6 * h:6 * h + 6], xt[:, 512 * h:512 * h + 512],
                    [xb], [st6_b])
        bnaggr(k, ag, st6, [st6_b], [ag_b])
        vcopy(k, mm_[:, t:t + 1], ag[:, 0:1], [ag_b], [mm_b])
        act(k, rstd[:, t:t + 1], ag[:, 1:2], AF.Ln, [ag_b, eps_b],
            [rstd_b], bias=eps_ap)
        act(k, rstd[:, t:t + 1], rstd[:, t:t + 1], AF.Exp, [rstd_b],
            [rstd_b], scale=-0.5)
        if tm is None and fm is None:
            continue
        if fm is not None and tm is None:
            stg, stg_b = stgs[i % 2]
        dst, dst_b = (tm, tm_b) if tm is not None else (stg, stg_b)
        dview = dst[:, t, :] if tm is not None else dst
        ts(k, dview, xt, mm_[:, t:t + 1], rstd[:, t:t + 1], ALU.subtract,
           ALU.mult, [xb, mm_b, rstd_b], [dst_b])
        if fm is not None:
            pt, pt_b = c.psumT if (i % 2 == 0) else c.psumT2
            for ch in range(NCH):
                tr(k, pt[:, 128 * ch:128 * ch + 128],
                   dview[:, 128 * ch:128 * ch + 128], c.ident, 128,
                   [dst_b, c.ident_b], [pt_b])
                evac(c, fm[:, ch, 128 * t:128 * t + 128], fm_b,
                     pt[:, 128 * ch:128 * ch + 128], pt_b)
    for i in range(2):
        if fm is not None and tm is None:
            c.ar.release(nm + f"_stg{i}")
        c.ar.release(nm + f"_6{i}")
        c.ar.release(nm + f"_ag{i}")
    if keep:
        return rstd, rstd_b, mm_, mm_b, nm
    c.ar.release(nm + "_r")
    c.ar.release(nm + "_m")
    return None

def ln_recompute_tile(c, dst, dst_b, xv, xbufs, stats, t):
    rstd, rstd_b, mm_, mm_b, nm = stats
    ts(c.k, dst, xv[:, t, :], mm_[:, t:t + 1], rstd[:, t:t + 1], ALU.subtract,
       ALU.mult, [xbufs[t], mm_b, rstd_b], [dst_b])

def release_ln_stats(c, stats):
    nm = stats[4]
    c.ar.release(nm + "_r")
    c.ar.release(nm + "_m")

def emit_post_ln_residual(c, xv, xbufs, z_ap, z_buf, nt, eps, nm):
    k = c.k
    bufs2 = []
    for i in range(2):
        st6, st6_b = c.ar.alloc(nm + f"_6{i}", [128, 12], F32)
        ag, ag_b = c.ar.alloc(nm + f"_ag{i}", [128, 2], F32)
        zr, zr_b = c.ar.alloc(nm + f"_zr{i}", [128, 1], F32)
        tmp, tmp_b = c.ar.alloc(nm + f"_t{i}", [128, 1024], F32)
        bufs2.append((st6, st6_b, ag, ag_b, zr, zr_b, tmp, tmp_b))
    eps_ap, eps_b = constcol(c, eps)
    for t in range(nt):
        st6, st6_b, ag, ag_b, zr, zr_b, tmp, tmp_b = bufs2[t % 2]
        zt = z_ap[:, t, :]
        for h in range(2):
            bnstats(k, st6[:, 6 * h:6 * h + 6], zt[:, 512 * h:512 * h + 512],
                    [z_buf], [st6_b])
        bnaggr(k, ag, st6, [st6_b], [ag_b])
        act(k, zr, ag[:, 1:2], AF.Ln, [ag_b, eps_b], [zr_b], bias=eps_ap)
        act(k, zr, zr, AF.Exp, [zr_b], [zr_b], scale=-0.5)
        ts(k, tmp, zt, ag[:, 0:1], zr, ALU.subtract, ALU.mult,
           [z_buf, ag_b, zr_b], [tmp_b])
        f_add = tt if t % 2 == 0 else ptt
        f_add(k, xv[:, t, :], xv[:, t, :], tmp, ALU.add, [tmp_b, xbufs[t]],
              [xbufs[t]])
    for i in range(2):
        for suf in (f"_6{i}", f"_ag{i}", f"_zr{i}", f"_t{i}"):
            c.ar.release(nm + suf)


def load_cast(c, dst_ap, dst_buf, src_ap, q="pool"):
    return dma(c.k, q, dst_ap, src_ap, (), [dst_buf])

def load_weight_T(c, name, w_dram, rows, cols):
    nr, ncl = rows // 128, cols // 128
    nat, nat_b = c.ar.alloc(name + "_nat", [128, nr, cols], BF16)
    load_cast(c, nat, nat_b, w_dram.rearrange("(a p) b -> p a b", p=128))
    wt, wt_b = c.ar.alloc(name, [128, ncl, rows], BF16)
    for a in range(nr):
        pt, pt_b = c.psumT if (a % 2 == 0) else c.psumT2
        for bb in range(ncl):
            slot = 128 * (bb % 8)
            tr(c.k, pt[:, slot:slot + 128], nat[:, a, 128 * bb:128 * bb + 128],
               c.ident, 128, [nat_b, c.ident_b], [pt_b])
            evac(c, wt[:, bb, 128 * a:128 * a + 128], wt_b,
                 pt[:, slot:slot + 128], pt_b)
    c.ar.release(name + "_nat")
    return wt, wt_b


def emit_l0a_weights(c, w):
    k = c.k
    c.w_uqT, c.w_uqT_b = load_weight_T(c, "w_uqT", w["W_uq"], 1024, 512)
    c.w_dqT, c.w_dqT_b = load_weight_T(c, "w_dqT", w["W_dq"], 512, 1024)
    c.w_dkvT, c.w_dkvT_b = load_weight_T(c, "w_dkvT", w["W_dkv"], 512, 1024)
    dqn, dqn_b = c.ar.alloc("dq_nat", [128, 4, 1024], BF16)
    load_cast(c, dqn, dqn_b, w["W_dq"].rearrange("(a p) b -> p a b", p=128))
    a1, a1_b = c.ar.alloc("a1", [128, 8, 1024], BF16)
    for ct in range(8):
        for g in range(2):
            pb, pb_b = c.psum[(ct * 2 + g) % 2]
            for ql in range(4):
                mm(k, pb[:, :], c.w_uqT[:, ql, 128 * ct:128 * ct + 128],
                   dqn[:, ql, 512 * g:512 * g + 512], ql == 0, ql == 3,
                   [c.w_uqT_b, dqn_b], [pb_b])
            evac(c, a1[:, ct, 512 * g:512 * g + 512], a1_b, pb[:, :], pb_b)
    c.ar.release("dq_nat")
    ukn, ukn_b = c.ar.alloc("uk_nat", [128, 8, 512], BF16)
    load_cast(c, ukn, ukn_b, w["W_uk"].rearrange("(a p) b -> p a b", p=128))
    c.ke, c.ke_b = c.ar.alloc("ke", [128, 4, 1024], BF16)
    for lt in range(4):
        for g in range(2):
            pb, pb_b = c.psum[(lt * 2 + g) % 2]
            for cc in range(8):
                mm(k, pb[:, :], ukn[:, cc, 128 * lt:128 * lt + 128],
                   a1[:, cc, 512 * g:512 * g + 512], cc == 0, cc == 7,
                   [ukn_b, a1_b], [pb_b])
            evac(c, c.ke[:, lt, 512 * g:512 * g + 512], c.ke_b, pb[:, :], pb_b,
                 scale=0.125)
    c.ar.release("uk_nat")
    c.ar.release("a1")
    woT, woT_b = load_weight_T(c, "woT", w["W_o"], 1024, 1024)
    uvn, uvn_b = c.ar.alloc("uv_nat", [128, 8, 512], BF16)
    load_cast(c, uvn, uvn_b, w["W_uv"].rearrange("(a p) b -> p a b", p=128))
    c.ve, c.ve_b = c.ar.alloc("ve", [128, 4, 1024], BF16)
    for lt in range(4):
        for g in range(2):
            pb, pb_b = c.psum[(lt * 2 + g) % 2]
            for cc in range(8):
                mm(k, pb[:, :], uvn[:, cc, 128 * lt:128 * lt + 128],
                   woT[:, cc, 512 * g:512 * g + 512], cc == 0, cc == 7,
                   [uvn_b, woT_b], [pb_b])
            evac(c, c.ve[:, lt, 512 * g:512 * g + 512], c.ve_b, pb[:, :], pb_b)
    c.ar.release("uv_nat")
    c.ar.release("woT")

def emit_l0a_seq(c, xv, xbufs, nt):
    k = c.k
    ng = nt // 4
    xnfm, xnfm_b = c.ar.alloc("l0a_xnfm", [128, NCH, nt * 128], BF16)
    emit_ln_pipe(c, xv, xbufs, nt, LN_EPS, "l0a1", fm=xnfm, fm_b=xnfm_b)
    ckv, ckv_b = c.ar.alloc("l0a_ckv", [128, 4, nt * 128], BF16)
    q1, q1_b = c.ar.alloc("l0a_q1", [128, 4, nt * 128], BF16)
    for (dst, dst_b, wt, wt_b) in ((ckv, ckv_b, c.w_dkvT, c.w_dkvT_b),
                                   (q1, q1_b, c.w_dqT, c.w_dqT_b)):
        for lt in range(4):
            for g in range(ng):
                pb, pb_b = c.psum[(lt * ng + g) % 2]
                for dc in range(NCH):
                    mm(k, pb[:, :], wt[:, dc, 128 * lt:128 * lt + 128],
                       xnfm[:, dc, 512 * g:512 * g + 512], dc == 0, dc == NCH - 1,
                       [wt_b, xnfm_b], [pb_b])
                evac(c, dst[:, lt, 512 * g:512 * g + 512], dst_b, pb[:, :], pb_b)
    c.ar.release("l0a_xnfm")
    z, z_b = c.ar.alloc("l0a_z", [128, nt, 1024], BF16)
    qp, qp_b = c.ar.alloc("l0a_qp", [128, nt * 128], BF16)
    kc, kc_b = c.ar.alloc("l0a_kc", [128, nt * 128], BF16)
    cv, cv_b = c.ar.alloc("l0a_cv", [128, nt, 130], BF16)
    ystg, ystg_b = c.ar.alloc("l0a_ystg", [65, 512], BF16)
    pexp = [[c.ar.alloc(f"l0a_pexp{s}{i}", [128, 512], BF16) for i in range(2)]
            for s in range(2)]
    vmemset(k, cv[:, :, 64:65], 1.0, [cv_b])
    vmemset(k, cv[:, :, 129:130], 1.0, [cv_b])
    for hp in range(NHEAD // 2):
        for g in range(ng):
            pb, pb_b = c.psum[4 + g % 2]
            for ql in range(4):
                mm(k, pb[:, :], c.w_uqT[:, ql, 128 * hp:128 * hp + 128],
                   q1[:, ql, 512 * g:512 * g + 512], ql == 0, ql == 3,
                   [c.w_uqT_b, q1_b], [pb_b])
            evac(c, qp[:, 512 * g:512 * g + 512], qp_b, pb[:, :], pb_b)
        for g in range(ng):
            pb, pb_b = c.psum[4 + g % 2]
            for lc in range(4):
                mm(k, pb[:, :], c.ke[:, lc, 128 * hp:128 * hp + 128],
                   ckv[:, lc, 512 * g:512 * g + 512], lc == 0, lc == 3,
                   [c.ke_b, ckv_b], [pb_b])
            evac(c, kc[:, 512 * g:512 * g + 512], kc_b, pb[:, :], pb_b)
        for kt in range(nt):
            pb, pb_b = c.psum[4 + kt % 2]
            for lc in range(4):
                mm(k, pb[:, 0:128], ckv[:, lc, 128 * kt:128 * kt + 128],
                   c.ve[:, lc, 128 * hp:128 * hp + 128], lc == 0, lc == 3,
                   [ckv_b, c.ve_b], [pb_b])
            evac(c, cv[:, kt, 0:64], cv_b, pb[:, 0:64], pb_b)
            evac(c, cv[:, kt, 65:129], cv_b, pb[:, 64:128], pb_b)
        for g in range(ng):
            nchunks = 4 * g + 4
            def score(ci, s, g=g):
                hd = slice(64 * s, 64 * s + 64)
                r = ci - 4 * g
                co = 128 * r if r >= 0 else 0
                sb, sb_b = (c.psum[0 + s], c.psum[4 + s])[ci % 2]
                mm(k, sb[:, co:512], kc[hd, 128 * ci:128 * ci + 128],
                   qp[hd, 512 * g + co:512 * g + 512], True, True,
                   [kc_b, qp_b], [sb_b])
            score(0, 0)
            score(0, 1)
            for ci in range(nchunks):
                r = ci - 4 * g
                co = 128 * r if r >= 0 else 0
                for s in range(2):
                    sb, sb_b = (c.psum[0 + s], c.psum[4 + s])[ci % 2]
                    pex, pex_b = pexp[s][ci % 2]
                    yb, yb_b = c.psum[2 + s]
                    if ci + 1 < nchunks and s == 1:
                        score(ci + 1, 0)
                        score(ci + 1, 1)
                    act(k, pex[:, co:512], sb[:, co:512], AF.Exp, [sb_b], [pex_b])
                    if r >= 0:
                        tt(k, pex[:, co:co + 128], pex[:, co:co + 128], c.mtri,
                           ALU.mult, [c.mtri_b, pex_b], [pex_b])
                    mm(k, yb[0:65, co:512], cv[:, ci, 65 * s:65 * s + 65],
                       pex[:, co:512], ci == 0, ci == nchunks - 1,
                       [cv_b, pex_b], [yb_b])
            for s in range(2):
                yb, yb_b = c.psum[2 + s]
                evac(c, ystg[:, :], ystg_b, yb[0:65, :], yb_b)
                pt, pt_b = c.psumT if (g % 2 == 0) else c.psumT2
                for j in range(4):
                    tr(k, pt[0:128, 130 * j:130 * j + 65],
                       ystg[0:65, 128 * j:128 * j + 128], c.ident, 65,
                       [ystg_b, c.ident_b], [pt_b])
                for j in range(4):
                    t_idx = 4 * g + j
                    rc, rc_b = c.ar.alloc("l0a_rc", [128, 1], F32)
                    vrecip(k, rc, pt[0:128, 130 * j + 64:130 * j + 65],
                           [pt_b], [rc_b])
                    ts(k, z[:, t_idx, 128 * hp + 64 * s:128 * hp + 64 * s + 64],
                       pt[0:128, 130 * j:130 * j + 64], rc, None, ALU.mult, None,
                       [pt_b, rc_b], [z_b])
                    c.ar.release("l0a_rc")
    for n in ("l0a_qp", "l0a_kc", "l0a_cv", "l0a_ystg", "l0a_pexp00",
              "l0a_pexp01", "l0a_pexp10", "l0a_pexp11", "l0a_ckv", "l0a_q1"):
        c.ar.release(n)
    emit_post_ln_residual(c, xv, xbufs, z, z_b, nt, LN_EPS, "l0a2")
    c.ar.release("l0a_z")

def release_l0a_weights(c):
    for nm in ("w_uqT", "w_dqT", "w_dkvT", "ke", "ve"):
        c.ar.release(nm)


def build_l0a(T=2048, nseq=2):
    nt = T // 128
    nc = bass.Bass("TRN2", target_bir_lowering=False)
    xin = nc.dram_tensor("xin", [nseq, T, D], F32, kind="ExternalInput").ap()
    xout = nc.dram_tensor("xout", [nseq, T, D], F32, kind="ExternalOutput").ap()
    w = {n: nc.dram_tensor(n, list(s), F32, kind="ExternalInput").ap()
         for n, s in [("W_dq", (QL, D)), ("W_uq", (D, QL)), ("W_dkv", (KVL, D)),
                      ("W_uk", (D, KVL)), ("W_uv", (D, KVL)), ("W_o", (D, D))]}
    with ExitStack() as stack:
        c = Ctx()
        c.nc = nc
        c.k = K(nc)
        c.ar = Arena(nc, stack, "arena", 52000)
        c.psum, c.psumT, c.psumT2 = make_psum(nc, stack)
        sems = {s: stack.enter_context(nc.semaphore(s)) for s in c.k.semnames}
        emit_constants(c)
        emit_l0a_weights(c, w)
        for b in range(nseq):
            xv, x_buf = c.ar.alloc("x", [128, nt, D], F32)
            xbufs = [x_buf] + [Buf(f"x{t}") for t in range(1, nt)]
            dma(c.k, "pool", xv, xin[b].rearrange("(t p) d -> p t d", p=128),
                (), xbufs)
            emit_l0a_seq(c, xv, xbufs, nt)
            dma(c.k, "pool", xout[b].rearrange("(t p) d -> p t d", p=128), xv,
                xbufs, ())
            c.ar.release("x", xbufs)
        c.k.barrier("pool", [])
        block = stack.enter_context(nc.Block())
        c.k.playback(block, sems)
    return nc


def kernel(**inputs):
    ncores = 8
    x = np.ascontiguousarray(inputs["L0_attn_x"])
    shards = [x[2 * cid:2 * cid + 2] for cid in range(ncores)]

    stages = [
        (build_l0a, "L0_attn_", ("W_dq", "W_uq", "W_dkv", "W_uk", "W_uv", "W_o")),
        (build_l0f, "L0_ffn_", ("ln_gamma", "ln_beta", "mu", "scale",
                                "w1", "b1", "w2", "b2")),
        (build_l1a, "L1_attn_", ("Wq", "bq", "Wk", "bk", "Wv", "bv", "Wo",
                                 "bo", "Wg", "bg", "Wc", "bc", "Ws", "bs")),
        (build_l1f, "L1_ffn_", ("w1", "b1", "w2", "b2")),
    ]
    for build, pref, wnames in stages:
        nc = build()
        in_maps = []
        for cid in range(ncores):
            m = {"xin": shards[cid]}
            for n in wnames:
                m[n] = inputs[pref + n]
            in_maps.append(m)
        res = run_bass_kernel_spmd(nc, in_maps, core_ids=list(range(ncores)))
        shards = [np.ascontiguousarray(r["xout"]) for r in res.results]
    return np.concatenate(shards, axis=0)

def bcast_row_f32(c, name, row_ap, row_buf, width):
    out, out_b = c.ar.alloc(name, [128, width], F32)
    for g in range((width + 511) // 512):
        w = min(512, width - 512 * g)
        pb, pb_b = c.psum[g % 2]
        mm(c.k, pb[0:128, 0:w], c.ones_rowf, row_ap[0:1, 512 * g:512 * g + w],
           True, True, [row_buf, c.ones_rowf_b], [pb_b])
        evac(c, out[:, 512 * g:512 * g + w], out_b, pb[0:128, 0:w], pb_b)
    return out, out_b

def emit_l0f_weights(c, w):
    k = c.k
    grow, grow_b = c.ar.alloc("l0f_grow", [1, D], F32)
    brow, brow_b = c.ar.alloc("l0f_brow", [1, D], F32)
    dma(k, "pool", grow, w["ln_gamma"].unsqueeze(0), (), [grow_b])
    dma(k, "pool", brow, w["ln_beta"].unsqueeze(0), (), [brow_b])
    c.gbc, c.gbc_b = bcast_row_f32(c, "l0f_gbc", grow, grow_b, D)
    c.bbc, c.bbc_b = bcast_row_f32(c, "l0f_bbc", brow, brow_b, D)
    c.ar.release("l0f_grow")
    c.ar.release("l0f_brow")
    c.mu, c.mu_b = c.ar.alloc("l0f_mu", [128, NCH, 256], BF16)
    load_cast(c, c.mu, c.mu_b, w["mu"].rearrange("(cc p) e s -> p cc (e s)", p=128))
    mu2, mu2_b = c.ar.alloc("l0f_mu2", [128, NCH, 256], BF16)
    act(k, mu2, c.mu, AF.Square, [c.mu_b], [mu2_b])
    pb, pb_b = c.psum[2]
    for cc in range(NCH):
        mm(k, pb[0:1, 0:256], c.ones_col, mu2[:, cc, :], cc == 0, cc == NCH - 1,
           [mu2_b, c.ones_col_b], [pb_b])
    c.ar.release("l0f_mu2")
    nr, nr_b = c.ar.alloc("l0f_nr", [1, 256], F32)
    evac(c, nr, nr_b, pb[0:1, 0:256], pb_b)
    scl, scl_b = c.ar.alloc("l0f_scl", [1, 1], F32)
    dma(k, "pool", scl, w["scale"].unsqueeze(0), (), [scl_b])
    one_ap, one_b = constcol(c, 0.0)
    act(k, nr, nr, AF.Ln, [nr_b, one_b], [nr_b], bias=one_ap[0:1, :])
    act(k, nr, nr, AF.Exp, [nr_b], [nr_b], scale=0.5)
    ts(k, nr, nr, 1e-12, None, ALU.max, None, [nr_b], [nr_b])
    vrecip(k, nr, nr, [nr_b], [nr_b])
    k.op("dve", lambda e, o=nr, i=nr, s=scl: e.tensor_scalar(o, i, s[0:1, 0:1],
         None, ALU.mult), reads=[nr_b, scl_b], writes=[nr_b])
    c.ar.release("l0f_scl")
    rsb, rsb_b = bcast_row_f32(c, "l0f_rsb", nr, nr_b, 256)
    c.ar.release("l0f_nr")
    for cc in range(NCH):
        tt(k, c.mu[:, cc, :], c.mu[:, cc, :], rsb, ALU.mult,
           [rsb_b, c.mu_b], [c.mu_b])
    c.ar.release("l0f_rsb")
    for (nm, src, rows, width) in (("l0f_b1T", w["b1"], 2, 256),
                                   ("l0f_b2T", w["b2"], 8, 1024)):
        natf, natf_b = c.ar.alloc(nm + "_nat", [16, width], F32)
        dma(k, "pool", natf, src, (), [natf_b])
        bt, bt_b = c.ar.alloc(nm, [128, rows, 16], F32)
        for j in range(rows):
            pt, pt_b = c.psum[3]
            trt = k.op("pe", lambda e, o=pt[0:128, 0:16],
                       i=natf[0:16, 128 * j:128 * j + 128],
                       id_=c.identf[0:16, 0:16]: e.transpose(o, i, id_),
                       reads=[natf_b, c.identf_b], writes=[pt_b])
            evac(c, bt[:, j, :], bt_b, pt[0:128, 0:16], pt_b)
        c.ar.release(nm + "_nat")
        if nm == "l0f_b1T":
            c.b1T, c.b1T_b = bt, bt_b
        else:
            c.b2T, c.b2T_b = bt, bt_b

def emit_l0f_seq(c, xv, xbufs, nt, w):
    k = c.k
    T = nt * 128
    xn2, xn2_b = c.ar.alloc("l0f_xn2", [128, nt, 1024], BF16)
    xub = [c.ar.alloc(f"l0f_xu{i}", [128, NCH, 128], BF16) for i in range(2)]
    ee, ee_b = c.ar.alloc("l0f_ee", [128, nt, 256], BF16)
    crow, crow_b = c.ar.alloc("l0f_crow", [128, nt], F32)
    scr = []
    for i in range(2):
        scr.append(tuple(
            c.ar.alloc(f"l0f_{nm}{i}", shp, dt2)
            for nm, shp, dt2 in (("6", [128, 12], F32), ("ag", [128, 2], F32),
                                 ("r1c", [128, 1], F32),
                                 ("stgf", [128, 1024], F32),
                                 ("tmpf", [128, 1024], F32),
                                 ("ssq", [128, 1], F32),
                                 ("junk", [128, 1024], BF16),
                                 ("stg", [128, 1024], BF16))))
    eps_ap, eps_b = constcol(c, LN_EPS)
    for t in range(nt):
        ((st6, st6_b), (ag, ag_b), (r1c, r1c_b), (stgf, stgf_b),
         (tmpf, tmpf_b), (ssq, ssq_b), (junk, junk_b), (stg, stg_b)) = scr[t % 2]
        xu, xu_b = xub[t % 2]
        xt = xv[:, t, :]
        for h in range(2):
            bnstats(k, st6[:, 6 * h:6 * h + 6], xt[:, 512 * h:512 * h + 512],
                    [xbufs[t]], [st6_b])
        bnaggr(k, ag, st6, [st6_b], [ag_b])
        act(k, r1c, ag[:, 1:2], AF.Ln, [ag_b, eps_b], [r1c_b], bias=eps_ap)
        act(k, r1c, r1c, AF.Exp, [r1c_b], [r1c_b], scale=-0.5)
        ts(k, ssq, ag[:, 1:2], r1c, r1c, ALU.mult, ALU.mult,
           [ag_b, r1c_b], [ssq_b])
        act(k, ssq, ssq, AF.Ln, [ssq_b, eps_b], [ssq_b], bias=eps_ap)
        act(k, ssq, ssq, AF.Exp, [ssq_b], [ssq_b], scale=-0.5)
        ts(k, r1c, r1c, ssq, None, ALU.mult, None, [ssq_b, r1c_b], [r1c_b])
        ts(k, tmpf, xt, ag[:, 0:1], r1c, ALU.subtract, ALU.mult,
           [xbufs[t], ag_b, r1c_b], [tmpf_b])
        ptt(k, tmpf, tmpf, c.gbc, ALU.mult, [c.gbc_b, tmpf_b], [tmpf_b])
        ptt(k, xn2[:, t, :], tmpf, c.bbc, ALU.add, [c.bbc_b, tmpf_b], [xn2_b])
        act(k, junk, xn2[:, t, :], AF.Square, [xn2_b], [junk_b, ssq_b],
            accum=ssq)
        act(k, ssq, ssq, AF.Ln, [ssq_b], [ssq_b])
        act(k, ssq, ssq, AF.Exp, [ssq_b], [ssq_b], scale=0.5)
        ts(k, ssq, ssq, 1e-12, None, ALU.max, None, [ssq_b], [ssq_b])
        vrecip(k, ssq, ssq, [ssq_b], [ssq_b])
        ts(k, stg, xn2[:, t, :], ssq, None, ALU.mult, None,
           [xn2_b, ssq_b], [stg_b])
        pt, pt_b = c.psumT if (t % 2 == 0) else c.psumT2
        for ch in range(NCH):
            tr(k, pt[:, 128 * ch:128 * ch + 128], stg[:, 128 * ch:128 * ch + 128],
               c.ident, 128, [stg_b, c.ident_b], [pt_b])
            evac(c, xu[:, ch, :], xu_b,
                 pt[:, 128 * ch:128 * ch + 128], pt_b)
        pb, pb_b = c.psum[t % 2]
        for ch in range(NCH):
            mm(k, pb[:, 0:256], xu[:, ch, :], c.mu[:, ch, :],
               ch == 0, ch == NCH - 1, [xu_b, c.mu_b], [pb_b])
        act(k, ee[:, t, :], pb[:, 0:256], AF.Exp, [pb_b], [ee_b, crow_b],
            accum=crow[:, t:t + 1])
    for i in range(2):
        for nm in ("6", "ag", "r1c", "stgf", "tmpf", "ssq", "junk", "stg"):
            c.ar.release(f"l0f_{nm}{i}")
        c.ar.release(f"l0f_xu{i}")
    cmb, cmb_b = c.ar.alloc("l0f_cmb", [128, nt, 256], BF16)
    vrecip(k, crow, crow, [crow_b], [crow_b])
    for t in range(nt):
        ts(k, cmb[:, t, :], ee[:, t, :], crow[:, t:t + 1], None, ALU.mult,
           None, [ee_b, crow_b], [cmb_b])
    pb, pb_b = c.psum[2]
    for t in range(nt):
        mm(k, pb[0:1, 0:256], c.ones_col, ee[:, t, :], t == 0, t == nt - 1,
           [c.ones_col_b, ee_b], [pb_b])
    dden, dden_b = c.ar.alloc("l0f_dden", [1, 256], F32)
    evac(c, dden, dden_b, pb[0:1, 0:256], pb_b)
    vrecip(k, dden, dden, [dden_b], [dden_b])
    ddb, ddb_b = bcast_row_f32(c, "l0f_ddb", dden, dden_b, 256)
    c.ar.release("l0f_dden")
    dsp, dsp_b = c.ar.alloc("l0f_dsp", [128, nt, 256], BF16)
    for t in range(nt):
        tt(k, dsp[:, t, :], ee[:, t, :], ddb, ALU.mult, [ee_b, ddb_b], [dsp_b])
    c.ar.release("l0f_ddb")
    c.ar.release("l0f_ee")
    c.ar.release("l0f_crow")
    si, si_b = c.ar.alloc("l0f_si", [128, NCH, 256], BF16)
    for db in range(NCH):
        pb, pb_b = c.psum[db % 4]
        for t in range(nt):
            mm(k, pb[:, 0:256], xn2[:, t, 128 * db:128 * db + 128],
               dsp[:, t, :], t == 0, t == nt - 1, [xn2_b, dsp_b], [pb_b])
        evac(c, si[:, db, :], si_b, pb[:, 0:256], pb_b)
    c.ar.release("l0f_dsp")
    so, so_b = c.ar.alloc("l0f_so", [128, NCH, 256], BF16)
    w1bufs = [c.ar.alloc(f"l0f_w1_{i}", [128, NCH, 256], BF16) for i in range(4)]
    w2bufs = [c.ar.alloc(f"l0f_w2_{i}", [128, 2, 1024], BF16) for i in range(4)]
    hfeb = [c.ar.alloc(f"l0f_hfe{i}", [128, 2, 16], BF16) for i in range(4)]
    for e in range(16):
        w1e, w1e_b = w1bufs[e % 4]
        w2e, w2e_b = w2bufs[e % 4]
        hfe, hfe_b = hfeb[e % 4]
        dma(k, "pool", w1e, w["w1"][e].rearrange("(cc p) h -> p cc h", p=128),
            (), [w1e_b])
        dma(k, "pool", w2e, w["w2"][e].rearrange("(hc p) d2 -> p hc d2", p=128),
            (), [w2e_b])
        for ht in range(2):
            pb, pb_b = c.psum[e % 4]
            for dc in range(NCH):
                mm(k, pb[:, 16 * ht:16 * ht + 16],
                   w1e[:, dc, 128 * ht:128 * ht + 128],
                   si[:, dc, 16 * e:16 * e + 16], dc == 0, dc == NCH - 1,
                   [w1e_b, si_b], [pb_b])
            act(k, hfe[:, ht, :], pb[:, 16 * ht:16 * ht + 16], AF.Gelu,
                [pb_b, c.b1T_b], [hfe_b], bias=c.b1T[:, ht, e:e + 1])
        for dt2 in range(NCH):
            pb, pb_b = c.psum[4 + e % 2]
            for hc in range(2):
                mm(k, pb[:, 16 * dt2:16 * dt2 + 16],
                   w2e[:, hc, 128 * dt2:128 * dt2 + 128], hfe[:, hc, :],
                   hc == 0, hc == 1, [w2e_b, hfe_b], [pb_b])
            evac(c, so[:, dt2, 16 * e:16 * e + 16], so_b,
                 pb[:, 16 * dt2:16 * dt2 + 16], pb_b,
                 bias=c.b2T[:, dt2, e:e + 1], scale=None, bias_buf=c.b2T_b)
    for i in range(4):
        c.ar.release(f"l0f_w1_{i}")
        c.ar.release(f"l0f_w2_{i}")
        c.ar.release(f"l0f_hfe{i}")
    c.ar.release("l0f_si")
    soe, soe_b = c.ar.alloc("l0f_soe", [128, 2, 1024], BF16)
    for db in range(NCH):
        pt, pt_b = c.psumT if (db % 2 == 0) else c.psumT2
        for eh in range(2):
            tr(k, pt[:, 128 * eh:128 * eh + 128],
               so[:, db, 128 * eh:128 * eh + 128], c.ident, 128,
               [so_b, c.ident_b], [pt_b])
            evac(c, soe[:, eh, 128 * db:128 * db + 128], soe_b,
                 pt[:, 128 * eh:128 * eh + 128], pt_b)
    c.ar.release("l0f_so")
    cmt, cmt_b = c.ar.alloc("l0f_cmt", [128, 2, T], BF16)
    for t in range(nt):
        pt, pt_b = c.psumT if (t % 2 == 0) else c.psumT2
        for eh in range(2):
            tr(k, pt[:, 128 * eh:128 * eh + 128],
               cmb[:, t, 128 * eh:128 * eh + 128], c.ident, 128,
               [cmb_b, c.ident_b], [pt_b])
            evac(c, cmt[:, eh, 128 * t:128 * t + 128], cmt_b,
                 pt[:, 128 * eh:128 * eh + 128], pt_b)
    c.ar.release("l0f_cmb")
    z, z_b = c.ar.alloc("l0f_z", [128, nt, 1024], BF16)
    for t in range(nt):
        for dg in range(2):
            pb, pb_b = c.psum[(t * 2 + dg) % 4]
            for ec in range(2):
                mm(k, pb[:, :], cmt[:, ec, 128 * t:128 * t + 128],
                   soe[:, ec, 512 * dg:512 * dg + 512], ec == 0, ec == 1,
                   [cmt_b, soe_b], [pb_b])
            evac(c, z[:, t, 512 * dg:512 * dg + 512], z_b, pb[:, :], pb_b)
    c.ar.release("l0f_cmt")
    c.ar.release("l0f_soe")
    c.ar.release("l0f_xn2")
    emit_post_ln_residual(c, xv, xbufs, z, z_b, nt, LN_EPS, "l0f2")
    c.ar.release("l0f_z")

def build_l0f(T=2048, nseq=2):
    nt = T // 128
    nc = bass.Bass("TRN2", target_bir_lowering=False)
    xin = nc.dram_tensor("xin", [nseq, T, D], F32, kind="ExternalInput").ap()
    xout = nc.dram_tensor("xout", [nseq, T, D], F32, kind="ExternalOutput").ap()
    w = {"ln_gamma": nc.dram_tensor("ln_gamma", [D], F32, kind="ExternalInput").ap(),
         "ln_beta": nc.dram_tensor("ln_beta", [D], F32, kind="ExternalInput").ap(),
         "mu": nc.dram_tensor("mu", [D, 16, 16], F32, kind="ExternalInput").ap(),
         "scale": nc.dram_tensor("scale", [1], F32, kind="ExternalInput").ap(),
         "w1": nc.dram_tensor("w1", [16, D, 256], F32, kind="ExternalInput").ap(),
         "b1": nc.dram_tensor("b1", [16, 256], F32, kind="ExternalInput").ap(),
         "w2": nc.dram_tensor("w2", [16, 256, D], F32, kind="ExternalInput").ap(),
         "b2": nc.dram_tensor("b2", [16, D], F32, kind="ExternalInput").ap()}
    with ExitStack() as stack:
        c = Ctx()
        c.nc = nc
        c.k = K(nc)
        c.ar = Arena(nc, stack, "arena", 52000)
        c.psum, c.psumT, c.psumT2 = make_psum(nc, stack)
        sems = {s: stack.enter_context(nc.semaphore(s)) for s in c.k.semnames}
        emit_constants(c)
        emit_l0f_weights(c, w)
        for b in range(nseq):
            xv, x_buf = c.ar.alloc("x", [128, nt, D], F32)
            xbufs = [x_buf] + [Buf(f"x{t}") for t in range(1, nt)]
            dma(c.k, "pool", xv, xin[b].rearrange("(t p) d -> p t d", p=128),
                (), xbufs)
            emit_l0f_seq(c, xv, xbufs, nt, w)
            dma(c.k, "pool", xout[b].rearrange("(t p) d -> p t d", p=128), xv,
                xbufs, ())
            c.ar.release("x", xbufs)
        c.k.barrier("pool", [])
        block = stack.enter_context(nc.Block())
        c.k.playback(block, sems)
    return nc

def release_l0f_weights(c):
    for nm in ("l0f_gbc", "l0f_bbc", "l0f_mu", "l0f_b1T", "l0f_b2T"):
        c.ar.release(nm)

def load_bias_col(c, name, vec_dram, n):
    rows = n // 128
    nat, nat_b = c.ar.alloc(name + "_nat", [rows, 128], F32)
    dma(c.k, "pool", nat, vec_dram.rearrange("(r p) -> r p", p=128), (), [nat_b])
    out, out_b = c.ar.alloc(name, [128, rows], F32)
    pb, pb_b = c.psum[3]
    c.k.op("pe", lambda e, o=pb[0:128, 0:rows], i=nat[0:rows, :],
           id_=c.identf[0:rows, 0:rows]: e.transpose(o, i, id_),
           reads=[nat_b, c.identf_b], writes=[pb_b])
    evac(c, out, out_b, pb[0:128, 0:rows], pb_b)
    c.ar.release(name + "_nat")
    return out, out_b

def emit_l1a_weights(c, w):
    k = c.k
    c.wg, c.wg_b = c.ar.alloc("l1a_Wg", [128, NCH, 3], BF16)
    load_cast(c, c.wg, c.wg_b, w["Wg"].rearrange("(cc p) g -> p cc g", p=128))
    wsr, wsr_b = c.ar.alloc("l1a_wsr", [1, D], F32)
    dma(k, "pool", wsr, w["Ws"].squeeze(1).unsqueeze(0), (), [wsr_b])
    c.wsB, c.wsB_b = bcast_row_f32(c, "l1a_wsB", wsr, wsr_b, D)
    sws, sws_b = c.ar.alloc("l1a_sws", [1, 1], F32)
    c.k.op("dve", lambda e, o=sws, i=wsr: e.reduce_sum(o, i,
           axis=mybir.AxisListType.X), reads=[wsr_b], writes=[sws_b])
    c.swsB, c.swsB_b = bcast_row_f32(c, "l1a_swsB", sws, sws_b, 1)
    c.ar.release("l1a_sws")
    c.ar.release("l1a_wsr")
    c.bqT, c.bqT_b = load_bias_col(c, "l1a_bqT", w["bq"], D)
    ts(k, c.bqT, c.bqT, 0.125, None, ALU.mult, None, [c.bqT_b], [c.bqT_b])
    c.bkT, c.bkT_b = load_bias_col(c, "l1a_bkT", w["bk"], D)
    c.bcT, c.bcT_b = load_bias_col(c, "l1a_bcT", w["bc"], D)
    bvr, bvr_b = c.ar.alloc("l1a_bvr", [1, D], F32)
    dma(k, "pool", bvr, w["bv"].unsqueeze(0), (), [bvr_b])
    c.bvB, c.bvB_b = bcast_row_f32(c, "l1a_bvB", bvr, bvr_b, D)
    c.ar.release("l1a_bvr")
    bor, bor_b = c.ar.alloc("l1a_bor", [1, D], F32)
    dma(k, "pool", bor, w["bo"].unsqueeze(0), (), [bor_b])
    c.boB, c.boB_b = bcast_row_f32(c, "l1a_boB", bor, bor_b, D)
    ts(k, c.boB, c.boB, 0.5, None, ALU.mult, None, [c.boB_b], [c.boB_b])
    c.ar.release("l1a_bor")
    bgr, bgr_b = c.ar.alloc("l1a_bgr", [1, 3], F32)
    dma(k, "pool", bgr, w["bg"].unsqueeze(0), (), [bgr_b])
    c.bgB, c.bgB_b = bcast_row_f32(c, "l1a_bgB", bgr, bgr_b, 3)
    c.ar.release("l1a_bgr")
    bsr, bsr_b = c.ar.alloc("l1a_bsr", [1, 1], F32)
    dma(k, "pool", bsr, w["bs"].unsqueeze(0), (), [bsr_b])
    c.bsB, c.bsB_b = bcast_row_f32(c, "l1a_bsB", bsr, bsr_b, 1)
    c.ar.release("l1a_bsr")


def _xn_fm_tiles(c, name, xv, xbufs, stats, t0, ntl):
    out, out_b = c.ar.alloc(name, [128, NCH, ntl * 128], BF16)
    stgs = [c.ar.alloc(name + f"_stg{j}", [128, 1024], BF16) for j in range(2)]
    for i in range(ntl):
        t = t0 + i
        stg, stg_b = stgs[i % 2]
        ln_recompute_tile(c, stg, stg_b, xv, xbufs, stats, t)
        pt, pt_b = c.psumT if (i % 2 == 0) else c.psumT2
        for ch in range(NCH):
            tr(c.k, pt[:, 128 * ch:128 * ch + 128],
               stg[:, 128 * ch:128 * ch + 128], c.ident, 128,
               [stg_b, c.ident_b], [pt_b])
            evac(c, out[:, ch, 128 * i:128 * i + 128], out_b,
                 pt[:, 128 * ch:128 * ch + 128], pt_b)
    c.ar.release(name + "_stg0")
    c.ar.release(name + "_stg1")
    return out, out_b

def _proj_fm(c, name, src_fm, src_b, wnat, wnat_b, cols, bias_col=None,
             bias_b=None, scale=None):
    out, out_b = c.ar.alloc(name, [128, NCH, cols], BF16)
    ngr = (cols + 511) // 512
    for dt_ in range(NCH):
        for g in range(ngr):
            wdt = min(512, cols - 512 * g)
            pb, pb_b = c.psum[(dt_ * ngr + g) % 4]
            for dc in range(NCH):
                mm(c.k, pb[:, 0:wdt], wnat[:, dc, 128 * dt_:128 * dt_ + 128],
                   src_fm[:, dc, 512 * g:512 * g + wdt], dc == 0, dc == NCH - 1,
                   [wnat_b, src_b], [pb_b])
            bias = None if bias_col is None else bias_col[:, dt_:dt_ + 1]
            evac(c, out[:, dt_, 512 * g:512 * g + wdt], out_b, pb[:, 0:wdt],
                 pb_b, scale=scale, bias=bias, bias_buf=bias_b)
    return out, out_b

def _vtm(c, name, src_fm, src_b, ntiles):
    out, out_b = c.ar.alloc(name, [128, ntiles, D], BF16)
    for t in range(ntiles):
        for dg in range(2):
            pb, pb_b = c.psum[(t * 2 + dg) % 4]
            for dc in range(NCH):
                mm(c.k, pb[:, :], src_fm[:, dc, 128 * t:128 * t + 128],
                   c.l1Wv[:, dc, 512 * dg:512 * dg + 512], dc == 0, dc == NCH - 1,
                   [src_b, c.l1Wv_b], [pb_b])
            stt(c.k, out[:, t, 512 * dg:512 * dg + 512], pb[:, :], 1.0,
                c.bvB[:, 512 * dg:512 * dg + 512], ALU.mult, ALU.add,
                [pb_b, c.bvB_b], [out_b])
    return out, out_b

def _attend_st(c, pref, qf, qf_b, kf, kf_b, vtm, vtm_b, ntok, yname):
    k = c.k
    nk = ntok // 128
    ngr = (ntok + 511) // 512
    pex, pex_b = c.ar.alloc(pref + "_pex", [128, nk, 512], BF16)
    sums, sums_b = c.ar.alloc(pref + "_sums", [1, ntok], F32)
    y, y_b = c.ar.alloc(yname, [128, NCH, ntok], BF16)
    for g in range(ngr):
        wdt = min(512, ntok - 512 * g)
        for kt in range(nk):
            sb, sb_b = c.psum[kt % 2]
            for dc in range(NCH):
                mm(k, sb[:, 0:wdt], kf[:, dc, 128 * kt:128 * kt + 128],
                   qf[:, dc, 512 * g:512 * g + wdt], dc == 0, dc == NCH - 1,
                   [kf_b, qf_b], [sb_b])
            act(k, pex[:, kt, 0:wdt], sb[:, 0:wdt], AF.Exp, [sb_b], [pex_b])
        pb, pb_b = c.psum[2]
        for kt in range(nk):
            mm(k, pb[0:1, 0:wdt], c.ones_col, pex[:, kt, 0:wdt],
               kt == 0, kt == nk - 1, [c.ones_col_b, pex_b], [pb_b])
        evac(c, sums[0:1, 512 * g:512 * g + wdt], sums_b, pb[0:1, 0:wdt], pb_b)
        for db in range(NCH):
            pb2, pb2_b = c.psum[3 + db % 3]
            for kt in range(nk):
                mm(k, pb2[:, 0:wdt], vtm[:, kt, 128 * db:128 * db + 128],
                   pex[:, kt, 0:wdt], kt == 0, kt == nk - 1,
                   [vtm_b, pex_b], [pb2_b])
            evac(c, y[:, db, 512 * g:512 * g + wdt], y_b, pb2[:, 0:wdt], pb2_b)
    c.ar.release(pref + "_pex")
    return y, y_b, sums, sums_b

def _grow_seg(c, name, gtm, gtm_b, col, t0, ntl):
    out, out_b = c.ar.alloc(name, [1, ntl * 128], F32)
    for i in range(ntl):
        pb, pb_b = c.psum[2 + i % 2]
        c.k.op("pe", lambda e, o=pb[0:1, 0:128], i_=gtm[:, t0 + i, col:col + 1],
               id_=c.identf: e.transpose(o, i_, id_),
               reads=[gtm_b, c.identf_b], writes=[pb_b])
        evac(c, out[0:1, 128 * i:128 * i + 128], out_b, pb[0:1, 0:128], pb_b)
    return out, out_b

def _scale_branch(c, pref, y, y_b, grow, grow_b, sums, sums_b, ntok):
    k = c.k
    sc, sc_b = c.ar.alloc(pref + "_sc", [1, ntok], F32)
    vrecip(k, sc, sums, [sums_b], [sc_b])
    tt(k, sc, sc, grow, ALU.mult, [grow_b, sc_b], [sc_b])
    scb, scb_b = bcast_row_f32(c, pref + "_scb", sc, sc_b, ntok)
    for db in range(NCH):
        tt(k, y[:, db, :], y[:, db, :], scb, ALU.mult, [scb_b, y_b], [y_b])
    c.ar.release(pref + "_scb")
    c.ar.release(pref + "_sc")

def _ln_tile_double(c, zt, zt_b, tmps, epsA_ap, epsA_b, epsB_ap, epsB_b):
    k = c.k
    zst6, zst6_b, zst, zst_b, zr, zr_b, zn, zn_b = tmps
    for h in range(2):
        bnstats(k, zst6[:, 6 * h:6 * h + 6], zt[:, 512 * h:512 * h + 512],
                [zt_b], [zst6_b])
    bnaggr(k, zst, zst6, [zst6_b], [zst_b])
    act(k, zr, zst[:, 1:2], AF.Ln, [zst_b, epsA_b], [zr_b], bias=epsA_ap)
    act(k, zr, zr, AF.Exp, [zr_b], [zr_b], scale=-0.5)
    ts(k, zn, zst[:, 1:2], zr, zr, ALU.mult, ALU.mult, [zst_b, zr_b], [zn_b])
    act(k, zn, zn, AF.Ln, [zn_b, epsB_b], [zn_b], bias=epsB_ap)
    act(k, zn, zn, AF.Exp, [zn_b], [zn_b], scale=-0.5)
    ts(k, zr, zr, zn, None, ALU.mult, None, [zn_b, zr_b], [zr_b])
    stt(k, zn, zst[:, 0:1], -1.0, zr, ALU.mult, ALU.mult, [zst_b, zr_b], [zn_b])
    act(k, zt, zt, AF.Identity, [zt_b, zr_b, zn_b], [zt_b], bias=zn, scale=zr)

b_dbg_idx = [0]

def emit_l1a_seq(c, xv, xbufs, nt, w):
    k = c.k
    T = nt * 128
    NW = T // 256
    NC4 = T // 4
    SELK = min(512, T)
    nct, nst = NC4 // 128, SELK // 128
    nzt = max(nct, nst)
    xnf, xnf_b = c.ar.alloc("l1a_xnf", [128, NCH, T], BF16)
    stats = emit_ln_pipe(c, xv, xbufs, nt, LN_EPS, "l1a1", fm=xnf,
                         fm_b=xnf_b, keep=True)
    gtm, gtm_b = c.ar.alloc("l1a_gtm", [128, nt, 3], F32)
    gt_, gt_b = c.ar.alloc("l1a_gt", [128, 3], F32)
    gsum, gsum_b = c.ar.alloc("l1a_gsum", [128, 1], F32)
    for t in range(nt):
        pb, pb_b = c.psum[t % 2]
        for dc in range(NCH):
            mm(k, pb[:, 0:3], xnf[:, dc, 128 * t:128 * t + 128], c.wg[:, dc, :],
               dc == 0, dc == NCH - 1, [xnf_b, c.wg_b], [pb_b])
        tt(k, gt_, pb[0:128, 0:3], c.bgB, ALU.add, [pb_b, c.bgB_b], [gt_b])
        k.op("act", lambda e, o=gt_, i=gt_: e.activation(o, i, AF.Sigmoid),
             reads=[gt_b], writes=[gt_b])
        k.op("dve", lambda e, o=gsum, i=gt_: e.reduce_sum(o, i,
             axis=mybir.AxisListType.X), reads=[gt_b], writes=[gsum_b])
        ts(k, gsum, gsum, 1e-6, None, ALU.add, None, [gsum_b], [gsum_b])
        vrecip(k, gsum, gsum, [gsum_b], [gsum_b])
        ts(k, gtm[:, t, :], gt_, gsum, None, ALU.mult, None,
           [gt_b, gsum_b], [gtm_b])
    c.ar.release("l1a_gt")
    c.ar.release("l1a_gsum")
    if getattr(c, "dbg_win", None) is not None and b_dbg_idx[0] == 0:
        dma(k, "pool", c.dbg_gtm, gtm, [gtm_b], ())
    s_tm, s_tm_b = c.ar.alloc("l1a_stm", [128, nt], F32)
    sstg, sstg_b = c.ar.alloc("l1a_sstg", [128, D], F32)
    for t in range(nt):
        tt(k, sstg, xv[:, t, :], c.wsB, ALU.mult, [xbufs[t], c.wsB_b], [sstg_b])
        k.op("dve", lambda e, o=s_tm[:, t:t + 1], i=sstg: e.reduce_sum(o, i,
             axis=mybir.AxisListType.X), reads=[sstg_b], writes=[s_tm_b])
    rstd_s, rstd_s_b, m_s, m_s_b, _nm = stats
    u_, u_b = c.ar.alloc("l1a_u", [128, nt], F32)
    ts(k, u_, m_s, c.swsB, None, ALU.mult, None, [m_s_b, c.swsB_b], [u_b])
    tt(k, s_tm, s_tm, u_, ALU.subtract, [u_b, s_tm_b], [s_tm_b])
    tt(k, s_tm, s_tm, rstd_s, ALU.mult, [rstd_s_b, s_tm_b], [s_tm_b])
    c.ar.release("l1a_wsB")
    c.ar.release("l1a_swsB")
    ts(k, s_tm, s_tm, c.bsB, None, ALU.add, None, [c.bsB_b, s_tm_b], [s_tm_b])
    c.ar.release("l1a_u")
    c.ar.release("l1a_sstg")
    srow, srow_b = c.ar.alloc("l1a_srow", [1, T], F32)
    for t in range(nt):
        pb, pb_b = c.psum[2 + t % 2]
        k.op("pe", lambda e, o=pb[0:1, 0:128], i=s_tm[:, t:t + 1],
             id_=c.identf: e.transpose(o, i, id_),
             reads=[s_tm_b, c.identf_b], writes=[pb_b])
        evac(c, srow[0:1, 128 * t:128 * t + 128], srow_b, pb[0:1, 0:128], pb_b)
    sbc, sbc_b = bcast_row_f32(c, "l1a_sbc", srow, srow_b, T)
    c.ar.release("l1a_srow")
    rank, rank_b = c.ar.alloc("l1a_rank", [128, nt], F32)
    cjunk, cjunk_b = c.ar.alloc("l1a_cjunk", [128, T], BF16)
    for t in range(nt):
        ts(k, cjunk, sbc, s_tm[:, t:t + 1], None, ALU.is_gt, None,
           [sbc_b, s_tm_b], [cjunk_b])
        act(k, cjunk, cjunk, AF.Identity, [cjunk_b], [cjunk_b, rank_b],
            accum=rank[:, t:t + 1])
    c.ar.release("l1a_cjunk")
    c.ar.release("l1a_sbc")
    c.ar.release("l1a_stm")
    cmpf, cmpf_b = c.ar.alloc("l1a_cmpf", [128, NCH, NC4], BF16)
    part, part_b = c.ar.alloc("l1a_part", [128, NCH, 512], F32)
    ngr = (NC4 + 511) // 512
    assert ngr == 1, "NC4 expected <= 512"
    for qtr in range(4):
        wcn, wcn_b = c.ar.alloc("l1a_wcn", [128, 8, D], BF16)
        load_cast(c, wcn, wcn_b,
                  w["Wc"][1024 * qtr:1024 * qtr + 1024]
                  .rearrange("(cc p) d2 -> p cc d2", p=128))
        for dt_ in range(NCH):
            pb, pb_b = c.psum[dt_ % 4]
            for ci in range(8):
                cc = 8 * qtr + ci
                j, dsub = cc // 8, cc % 8
                mm(k, pb[:, 0:NC4], wcn[:, ci, 128 * dt_:128 * dt_ + 128],
                   xnf[:, dsub, j:j + 4 * (NC4 - 1) + 1:4], ci == 0, ci == 7,
                   [wcn_b, xnf_b], [pb_b])
            if qtr == 0:
                evac(c, part[:, dt_, 0:NC4], part_b, pb[:, 0:NC4], pb_b)
            elif qtr < 3:
                stt(k, part[:, dt_, 0:NC4], pb[:, 0:NC4], 1.0,
                    part[:, dt_, 0:NC4], ALU.mult, ALU.add, [pb_b, part_b],
                    [part_b])
            else:
                stt(k, part[:, dt_, 0:NC4], pb[:, 0:NC4], 1.0,
                    part[:, dt_, 0:NC4], ALU.mult, ALU.add, [pb_b, part_b],
                    [part_b])
                act(k, cmpf[:, dt_, 0:NC4], part[:, dt_, 0:NC4], AF.Identity,
                    [part_b, c.bcT_b], [cmpf_b], bias=c.bcT[:, dt_:dt_ + 1])
        c.ar.release("l1a_wcn")
    c.ar.release("l1a_part")
    c.ar.release("l1a_xnf")
    xnt, xnt_b = c.ar.alloc("l1a_xnt", [128, nt, D], BF16)
    for t in range(nt):
        ln_recompute_tile(c, xnt[:, t, :], xnt_b, xv, xbufs, stats, t)
    for nm in ("Wq", "Wk", "Wv"):
        ap, b = c.ar.alloc("l1a_" + nm, [128, NCH, D], BF16)
        load_cast(c, ap, b, w[nm].rearrange("(cc p) d2 -> p cc d2", p=128))
        setattr(c, "l1" + nm, ap)
        setattr(c, "l1" + nm + "_b", b)
    qc, qc_b = _proj_fm(c, "l1a_qc", cmpf, cmpf_b, c.l1Wq, c.l1Wq_b, NC4,
                        bias_col=c.bqT, bias_b=c.bqT_b, scale=0.125)
    kc_, kc_b2 = _proj_fm(c, "l1a_kc", cmpf, cmpf_b, c.l1Wk, c.l1Wk_b, NC4,
                          bias_col=c.bkT, bias_b=c.bkT_b)
    vc, vc_b = _vtm(c, "l1a_vc", cmpf, cmpf_b, NC4 // 128)
    c.ar.release("l1a_cmpf")
    cof, cof_b, csums, csums_b = _attend_st(c, "l1a_c", qc, qc_b, kc_, kc_b2,
                                            vc, vc_b, NC4, "l1a_cof")
    g0s, g0s_b = _grow_seg(c, "l1a_g0s", gtm, gtm_b, 0, 0, nct)
    _scale_branch(c, "l1a_cs", cof, cof_b, g0s, g0s_b, csums, csums_b, NC4)
    if getattr(c, "dbg_win", None) is not None and b_dbg_idx[0] == 0:
        dma(k, "pool", c.dbg_cof, cof, [cof_b], ())
    c.ar.release("l1a_g0s")
    c.ar.release("l1a_c_sums")
    c.ar.release("l1a_qc")
    c.ar.release("l1a_kc")
    c.ar.release("l1a_vc")
    mtm, mtm_b = c.ar.alloc("l1a_m", [128, nt], BF16)
    ts(k, mtm, rank, float(SELK), None, ALU.is_lt, None, [rank_b], [mtm_b])
    c.ar.release("l1a_rank")
    pb, pb_b = c.psum[0]
    mm(k, pb[:, 0:nt], c.striu, mtm, True, True, [c.striu_b, mtm_b], [pb_b])
    pos, pos_b = c.ar.alloc("l1a_pos", [128, nt], F32)
    evac(c, pos, pos_b, pb[0:128, 0:nt], pb_b)
    pb, pb_b = c.psum[1]
    mm(k, pb[0:1, 0:nt], c.ones_col, mtm, True, True, [c.ones_col_b, mtm_b], [pb_b])
    crow, crow_b = c.ar.alloc("l1a_crow", [1, nt], F32)
    evac(c, crow, crow_b, pb[0:1, 0:nt], pb_b)
    cnt_col, cnt_col_b = c.ar.alloc("l1a_cnt", [nt, 1], BF16)
    pb2, pb2_b = c.psum[2]
    k.op("pe", lambda e, o=pb2[0:nt, 0:1], i=crow[0:1, 0:nt],
         id_=c.identf[0:1, 0:1]: e.transpose(o, i, id_),
         reads=[crow_b, c.identf_b], writes=[pb2_b])
    evac(c, cnt_col, cnt_col_b, pb2[0:nt, 0:1], pb2_b)
    c.ar.release("l1a_crow")
    pb, pb_b = c.psum[3]
    mm(k, pb[0:1, 0:nt], cnt_col, c.striu[0:nt, 0:nt], True, True,
       [cnt_col_b, c.striu_b], [pb_b])
    offr, offr_b = c.ar.alloc("l1a_offr", [1, nt], F32)
    evac(c, offr, offr_b, pb[0:1, 0:nt], pb_b)
    offb, offb_b = bcast_row_f32(c, "l1a_offb", offr, offr_b, nt)
    tt(k, pos, pos, offb, ALU.add, [offb_b, pos_b], [pos_b])
    c.ar.release("l1a_offb")
    c.ar.release("l1a_offr")
    c.ar.release("l1a_cnt")
    tmpm, tmpm_b = c.ar.alloc("l1a_tmpm", [128, nt], F32)
    ts(k, tmpm, mtm, -100000.0, 100000.0, ALU.mult, ALU.add, [mtm_b], [tmpm_b])
    tt(k, pos, pos, tmpm, ALU.add, [tmpm_b, pos_b], [pos_b])
    c.ar.release("l1a_tmpm")
    c.ar.release("l1a_m")
    gts, gts_b = c.ar.alloc("l1a_gts", [128, nt, SELK], BF16)
    for t in range(nt):
        ts(k, gts[:, t, :], c.io512[:, 0:SELK], pos[:, t:t + 1], None,
           ALU.is_equal, None, [c.io512_b, pos_b], [gts_b])
    if getattr(c, "dbg_win", None) is not None and b_dbg_idx[0] == 0:
        dma(k, "pool", c.dbg_pos, pos, [pos_b], ())
    c.ar.release("l1a_pos")
    self_, self_b = c.ar.alloc("l1a_self", [128, NCH, SELK], BF16)
    for half in range(2):
        for db in range(4 * half, 4 * half + 4):
            pb, pb_b = c.psum[db % 4]
            for t in range(nt):
                mm(k, pb[:, 0:SELK], xnt[:, t, 128 * db:128 * db + 128],
                   gts[:, t, :], t == 0, t == nt - 1, [xnt_b, gts_b], [pb_b])
            evac(c, self_[:, db, :], self_b, pb[:, 0:SELK], pb_b)
    c.ar.release("l1a_gts")
    c.ar.release("l1a_xnt")
    qs, qs_b = _proj_fm(c, "l1a_qs", self_, self_b, c.l1Wq, c.l1Wq_b, SELK,
                        bias_col=c.bqT, bias_b=c.bqT_b, scale=0.125)
    ks_, ks_b = _proj_fm(c, "l1a_ks", self_, self_b, c.l1Wk, c.l1Wk_b, SELK,
                         bias_col=c.bkT, bias_b=c.bkT_b)
    vs, vs_b = _vtm(c, "l1a_vs", self_, self_b, SELK // 128)
    c.ar.release("l1a_self")
    sof, sof_b, ssums, ssums_b = _attend_st(c, "l1a_s", qs, qs_b, ks_, ks_b,
                                            vs, vs_b, SELK, "l1a_sof")
    g1s, g1s_b = _grow_seg(c, "l1a_g1s", gtm, gtm_b, 1, 0, nst)
    _scale_branch(c, "l1a_ss", sof, sof_b, g1s, g1s_b, ssums, ssums_b, SELK)
    if getattr(c, "dbg_win", None) is not None and b_dbg_idx[0] == 0:
        dma(k, "pool", c.dbg_sof, sof, [sof_b], ())
    c.ar.release("l1a_g1s")
    c.ar.release("l1a_s_sums")
    c.ar.release("l1a_qs")
    c.ar.release("l1a_ks")
    c.ar.release("l1a_vs")
    z1, z1_b = c.ar.alloc("l1a_z1", [128, nzt, D], BF16)
    for half, (src_f, src_fb, ntl) in enumerate(((cof, cof_b, nct),
                                                 (sof, sof_b, nst))):
        won, won_b = c.ar.alloc("l1a_wonA", [128, NCH, D], BF16)
        load_cast(c, won, won_b,
                  w["Wo"][1024 * half:1024 * half + 1024]
                  .rearrange("(cc p) d2 -> p cc d2", p=128))
        for t in range(nzt):
            for dg in range(2):
                if t >= ntl:
                    if half == 0:
                        pass
                    continue
                pb, pb_b = c.psum[(t * 2 + dg) % 4]
                for cc in range(NCH):
                    mm(k, pb[:, :], src_f[:, cc, 128 * t:128 * t + 128],
                       won[:, cc, 512 * dg:512 * dg + 512], cc == 0,
                       cc == NCH - 1, [src_fb, won_b], [pb_b])
                if half == 0 or t >= nct:
                    evac(c, z1[:, t, 512 * dg:512 * dg + 512], z1_b,
                         pb[:, :], pb_b)
                else:
                    stt(k, z1[:, t, 512 * dg:512 * dg + 512], pb[:, :], 1.0,
                        z1[:, t, 512 * dg:512 * dg + 512], ALU.mult, ALU.add,
                        [pb_b, z1_b], [z1_b])
        c.ar.release("l1a_wonA")
    c.ar.release("l1a_cof")
    c.ar.release("l1a_sof")
    winf, winf_b = c.ar.alloc("l1a_winf", [128, NCH, T], BF16)
    pwin, pwin_b = c.ar.alloc("l1a_pw", [128, 2, 256], BF16)
    for cw in range(NW // 2):
        t0 = 2 * cw
        xnw, xnw_b = _xn_fm_tiles(c, "l1a_xnw", xv, xbufs, stats, t0, 3)
        qw, qw_b = _proj_fm(c, "l1a_qw", xnw, xnw_b, c.l1Wq, c.l1Wq_b, 384,
                            bias_col=c.bqT, bias_b=c.bqT_b, scale=0.125)
        kw, kw_b = _proj_fm(c, "l1a_kw", xnw, xnw_b, c.l1Wk, c.l1Wk_b, 384,
                            bias_col=c.bkT, bias_b=c.bkT_b)
        vw, vw_b = _vtm(c, "l1a_vw", xnw, xnw_b, 3)
        if getattr(c, "dbg_win", None) is not None and b_dbg_idx[0] == 0:
            dma(k, "pool", c.dbg_xnw[cw], xnw, [xnw_b], ())
            dma(k, "pool", c.dbg_qw[cw], qw, [qw_b], ())
        c.ar.release("l1a_xnw")
        wsum, wsum_b = c.ar.alloc("l1a_wsum", [1, 512], F32)
        for wl in range(2):
            for kt in range(2):
                sb, sb_b = c.psum[kt % 2]
                for dc in range(NCH):
                    mm(k, sb[:, 0:256],
                       kw[:, dc, 128 * wl + 128 * kt:128 * wl + 128 * kt + 128],
                       qw[:, dc, 128 * wl:128 * wl + 256], dc == 0,
                       dc == NCH - 1, [kw_b, qw_b], [sb_b])
                act(k, pwin[:, kt, :], sb[:, 0:256], AF.Exp, [sb_b], [pwin_b])
            pb, pb_b = c.psum[2]
            for kt in range(2):
                mm(k, pb[0:1, 0:256], c.ones_col, pwin[:, kt, :], kt == 0,
                   kt == 1, [c.ones_col_b, pwin_b], [pb_b])
            evac(c, wsum[0:1, 256 * wl:256 * wl + 256], wsum_b,
                 pb[0:1, 0:256], pb_b)
            for db in range(NCH):
                pb2, pb2_b = c.psum[3 + db % 3]
                for kt in range(2):
                    mm(k, pb2[:, 0:256], vw[:, wl + kt, 128 * db:128 * db + 128],
                       pwin[:, kt, :], kt == 0, kt == 1, [vw_b, pwin_b], [pb2_b])
                evac(c, winf[:, db, 512 * cw + 256 * wl:512 * cw + 256 * wl + 256],
                     winf_b, pb2[:, 0:256], pb2_b)
        c.ar.release("l1a_qw")
        c.ar.release("l1a_kw")
        c.ar.release("l1a_vw")
        if getattr(c, "dbg_win", None) is not None and b_dbg_idx[0] == 0:
            dma(k, "pool", c.dbg_win[cw],
                winf[:, :, 512 * cw:512 * cw + 512], [winf_b], ())
            dma(k, "pool", c.dbg_wsum[cw], wsum, [wsum_b], ())
        g2s, g2s_b = _grow_seg(c, "l1a_g2s", gtm, gtm_b, 2, 4 * cw, 4)
        sc, sc_b = c.ar.alloc("l1a_wsc", [1, 512], F32)
        vrecip(k, sc, wsum, [wsum_b], [sc_b])
        tt(k, sc, sc, g2s, ALU.mult, [g2s_b, sc_b], [sc_b])
        scb, scb_b = bcast_row_f32(c, "l1a_wscb", sc, sc_b, 512)
        for db in range(NCH):
            tt(k, winf[:, db, 512 * cw:512 * cw + 512],
               winf[:, db, 512 * cw:512 * cw + 512], scb, ALU.mult,
               [scb_b, winf_b], [winf_b])
        c.ar.release("l1a_wscb")
        c.ar.release("l1a_wsc")
        c.ar.release("l1a_g2s")
        c.ar.release("l1a_wsum")
    c.ar.release("l1a_pw")
    for nm in ("l1a_Wq", "l1a_Wk", "l1a_Wv"):
        c.ar.release(nm)
    epi = []
    for i in range(2):
        xrec, xrec_b = c.ar.alloc(f"l1a_xrec{i}", [128, D], F32)
        base, base_b = c.ar.alloc(f"l1a_base{i}", [128, D], F32)
        zt, zt_b = c.ar.alloc(f"l1a_zt{i}", [128, D], F32)
        zst6, zst6_b = c.ar.alloc(f"l1a_zst6{i}", [128, 12], F32)
        zst, zst_b = c.ar.alloc(f"l1a_zst{i}", [128, 2], F32)
        zr, zr_b = c.ar.alloc(f"l1a_zr{i}", [128, 1], F32)
        zn, zn_b = c.ar.alloc(f"l1a_zn{i}", [128, 1], F32)
        epi.append((xrec, xrec_b, base, base_b, zt, zt_b,
                    (zst6, zst6_b, zst, zst_b, zr, zr_b, zn, zn_b)))
    eps6_ap, eps6_b = constcol(c, 1e-6)
    eps5_ap, eps5_b = constcol(c, LN_EPS)
    won2, won2_b = c.ar.alloc("l1a_wonB", [128, NCH, D], BF16)
    load_cast(c, won2, won2_b,
              w["Wo"][2048:3072].rearrange("(cc p) d2 -> p cc d2", p=128))
    for t in range(nt):
        xrec, xrec_b, base, base_b, zt, zt_b, tmps = epi[t % 2]
        ln_recompute_tile(c, xrec, xrec_b, xv, xbufs, stats, t)
        stt(k, base, xrec, 0.5, c.boB, ALU.mult, ALU.add,
            [xrec_b, c.boB_b], [base_b])
        if t < nzt:
            stt(k, base, z1[:, t, :], 0.5, base, ALU.mult, ALU.add,
                [z1_b, base_b], [base_b])
        for dg in range(2):
            pb, pb_b = c.psum[(2 if t % 2 else 4) + dg]
            for cc in range(NCH):
                mm(k, pb[:, :], winf[:, cc, 128 * t:128 * t + 128],
                   won2[:, cc, 512 * dg:512 * dg + 512], cc == 0,
                   cc == NCH - 1, [winf_b, won2_b], [pb_b])
            stt(k, zt[:, 512 * dg:512 * dg + 512], pb[:, :], 0.5,
                base[:, 512 * dg:512 * dg + 512], ALU.mult, ALU.add,
                [pb_b, base_b], [zt_b])
        _ln_tile_double(c, zt, zt_b, tmps, eps6_ap, eps6_b, eps5_ap, eps5_b)
        tt(k, xv[:, t, :], xv[:, t, :], zt, ALU.add, [zt_b, xbufs[t]],
           [xbufs[t]])
    for nm in ("l1a_wonB", "l1a_z1", "l1a_gtm", "l1a_winf"):
        c.ar.release(nm)
    for i in range(2):
        for nm in (f"l1a_xrec{i}", f"l1a_base{i}", f"l1a_zt{i}",
                   f"l1a_zst6{i}", f"l1a_zst{i}", f"l1a_zr{i}", f"l1a_zn{i}"):
            c.ar.release(nm)
    release_ln_stats(c, stats)

def build_l1a(T=2048, nseq=2, debug=False):
    nt = T // 128
    nc = bass.Bass("TRN2", target_bir_lowering=False)
    xin = nc.dram_tensor("xin", [nseq, T, D], F32, kind="ExternalInput").ap()
    xout = nc.dram_tensor("xout", [nseq, T, D], F32, kind="ExternalOutput").ap()
    names = [("Wq", (D, D)), ("bq", (D,)), ("Wk", (D, D)), ("bk", (D,)),
             ("Wv", (D, D)), ("bv", (D,)), ("Wo", (3 * D, D)), ("bo", (D,)),
             ("Wg", (D, 3)), ("bg", (3,)), ("Wc", (4 * D, D)), ("bc", (D,)),
             ("Ws", (D, 1)), ("bs", (1,))]
    w = {n: nc.dram_tensor(n, list(s), F32, kind="ExternalInput").ap()
         for n, s in names}
    with ExitStack() as stack:
        c = Ctx()
        c.nc = nc
        c.k = K(nc)
        c.ar = Arena(nc, stack, "arena", 52000)
        c.psum, c.psumT, c.psumT2 = make_psum(nc, stack)
        sems = {s: stack.enter_context(nc.semaphore(s)) for s in c.k.semnames}
        emit_constants(c)
        emit_l1a_weights(c, w)
        c.dbg_win = None
        if debug:
            NWh = (T // 256) // 2
            c.dbg_win = [nc.dram_tensor(f"dbg_win{i}", [128, NCH, 512], F32,
                                        kind="ExternalOutput").ap()
                         for i in range(NWh)]
            c.dbg_wsum = [nc.dram_tensor(f"dbg_wsum{i}", [1, 512], F32,
                                         kind="ExternalOutput").ap()
                          for i in range(NWh)]
            c.dbg_xnw = [nc.dram_tensor(f"dbg_xnw{i}", [128, NCH, 384], F32,
                                        kind="ExternalOutput").ap()
                         for i in range(NWh)]
            c.dbg_qw = [nc.dram_tensor(f"dbg_qw{i}", [128, NCH, 384], F32,
                                       kind="ExternalOutput").ap()
                        for i in range(NWh)]
            c.dbg_gtm = nc.dram_tensor("dbg_gtm", [128, T // 128, 3], F32,
                                       kind="ExternalOutput").ap()
            c.dbg_cof = nc.dram_tensor("dbg_cof", [128, NCH, T // 4], F32,
                                       kind="ExternalOutput").ap()
            c.dbg_sof = nc.dram_tensor("dbg_sof", [128, NCH, min(512, T)], F32,
                                       kind="ExternalOutput").ap()
            c.dbg_pos = nc.dram_tensor("dbg_pos", [128, T // 128], F32,
                                       kind="ExternalOutput").ap()
        for b in range(nseq):
            b_dbg_idx[0] = b
            xv, x_buf = c.ar.alloc("x", [128, nt, D], F32)
            xbufs = [x_buf] + [Buf(f"x{t}") for t in range(1, nt)]
            dma(c.k, "pool", xv, xin[b].rearrange("(t p) d -> p t d", p=128),
                (), xbufs)
            emit_l1a_seq(c, xv, xbufs, nt, w)
            dma(c.k, "pool", xout[b].rearrange("(t p) d -> p t d", p=128), xv,
                xbufs, ())
            c.ar.release("x", xbufs)
        c.k.barrier("pool", [])
        block = stack.enter_context(nc.Block())
        c.k.playback(block, sems)
    return nc

def release_l1a_weights(c):
    for nm in ("l1a_Wg", "l1a_bqT", "l1a_bkT", "l1a_bcT", "l1a_bvB", "l1a_boB",
               "l1a_bgB", "l1a_bsB"):
        c.ar.release(nm)

def emit_l1f_weights(c, w):
    k = c.k
    c.f_b1T, c.f_b1T_b = load_bias_col(c, "l1f_b1T", w["b1"], 4096)
    b2r, b2r_b = c.ar.alloc("l1f_b2r", [1, D], F32)
    dma(k, "pool", b2r, w["b2"].unsqueeze(0), (), [b2r_b])
    c.f_b2B, c.f_b2B_b = bcast_row_f32(c, "l1f_b2B", b2r, b2r_b, D)
    c.ar.release("l1f_b2r")

def _binarize_w1_tile(c, w, ht, bwt, bwt_b):
    k = c.k
    sfx = str(ht % 2)
    wt, wt_b = c.ar.alloc("l1f_w1nat" + sfx, [128, D], BF16)
    load_cast(c, wt, wt_b, w["w1"][128 * ht:128 * ht + 128, :])
    aj, aj_b = c.ar.alloc("l1f_aj" + sfx, [128, D], BF16)
    al, al_b = c.ar.alloc("l1f_al" + sfx, [128, 1], F32)
    act(k, aj, wt, AF.Abs, [wt_b], [aj_b, al_b], accum=al)
    act(k, aj, wt, AF.Sign, [wt_b], [aj_b])
    bw, bw_b = c.ar.alloc("l1f_bw" + sfx, [128, D], BF16)
    ts(k, bw, aj, al, 1.0 / D, ALU.mult, ALU.mult, [aj_b, al_b], [bw_b])
    pt, pt_b = c.psumT if (ht % 2 == 0) else c.psumT2
    for dc in range(NCH):
        tr(k, pt[:, 128 * dc:128 * dc + 128], bw[:, 128 * dc:128 * dc + 128],
           c.ident, 128, [bw_b, c.ident_b], [pt_b])
        evac(c, bwt[:, dc, :], bwt_b, pt[:, 128 * dc:128 * dc + 128], pt_b)
    c.ar.release("l1f_w1nat" + sfx)
    c.ar.release("l1f_aj" + sfx)
    c.ar.release("l1f_al" + sfx)
    c.ar.release("l1f_bw" + sfx)

def _binarize_w2(c, w, bw2_dram):
    k = c.k
    c.f_bw2_ticks = []
    for dt_ in range(NCH):
        wt, wt_b = c.ar.alloc("l1f_w2nat", [128, 4096], BF16)
        load_cast(c, wt, wt_b, w["w2"][128 * dt_:128 * dt_ + 128, :])
        aj, aj_b = c.ar.alloc("l1f_aj2", [128, 4096], BF16)
        al, al_b = c.ar.alloc("l1f_al2", [128, 1], F32)
        act(k, aj, wt, AF.Abs, [wt_b], [aj_b, al_b], accum=al)
        act(k, aj, wt, AF.Sign, [wt_b], [aj_b])
        bw, bw_b = c.ar.alloc("l1f_bw2", [128, 4096], BF16)
        ts(k, bw, aj, al, 1.0 / 4096, ALU.mult, ALU.mult, [aj_b, al_b], [bw_b])
        stg, stg_b = c.ar.alloc("l1f_stg2", [128, 32, 128], BF16)
        for hc in range(32):
            pt, pt_b = c.psumT if (hc % 2 == 0) else c.psumT2
            tr(k, pt[:, 0:128], bw[:, 128 * hc:128 * hc + 128], c.ident, 128,
               [bw_b, c.ident_b], [pt_b])
            evac(c, stg[:, hc, :], stg_b, pt[:, 0:128], pt_b)
        t_ = dma(k, "pool",
                 bw2_dram[:, :, 128 * dt_:128 * dt_ + 128]
                 .rearrange("h p d2 -> p h d2"),
                 stg, [stg_b], ())
        c.f_bw2_ticks.append(t_)
        for nm in ("l1f_w2nat", "l1f_aj2", "l1f_al2", "l1f_bw2", "l1f_stg2"):
            c.ar.release(nm)

def emit_l1f_seq(c, xv, xbufs, nt, w, bw1_dram, bw2_dram, first):
    k = c.k
    T = nt * 128
    nhalf = nt // 8 if nt >= 8 else 1
    tiles_per_half = nt // nhalf
    stats = emit_ln_pipe(c, xv, xbufs, nt, LN_EPS, "l1f1", keep=True)
    if not hasattr(c, "f_bw1_ticks"):
        c.f_bw1_ticks = {}
    z, z_b = c.ar.alloc("l1f_z", [128, nt, D], BF16)
    bwtb = [c.ar.alloc(f"l1f_bwt{i}", [128, NCH, 128], BF16) for i in range(4)]
    for half in range(nhalf):
        t0 = half * tiles_per_half
        Th = tiles_per_half * 128
        xnf, xnf_b = _xn_fm_tiles(c, "l1f_xnf", xv, xbufs, stats, t0,
                                  tiles_per_half)
        hf, hf_b = c.ar.alloc("l1f_h", [128, 32, Th], BF16)
        ngr = Th // 512
        for ht in range(32):
            bwt, bwt_b = bwtb[ht % (2 if first and half == 0 else 4)]
            if first and half == 0:
                _binarize_w1_tile(c, w, ht, bwt, bwt_b)
                c.f_bw1_ticks[ht] = dma(
                    k, "pool",
                    bw1_dram[:, :, 128 * ht:128 * ht + 128]
                    .rearrange("cc p h -> p cc h"), bwt, [bwt_b], ())
            else:
                c.k.op("sp", lambda e, o=bwt,
                       i=bw1_dram[:, :, 128 * ht:128 * ht + 128]
                       .rearrange("cc p h -> p cc h"):
                       e.dma_start(out=o, in_=i), reads=(), writes=[bwt_b],
                       dma=True, deps=(c.f_bw1_ticks[ht],), dur=2200)
            for g in range(ngr):
                pb, pb_b = c.psum[(ht * ngr + g) % 4]
                for dc in range(NCH):
                    mm(k, pb[:, :], bwt[:, dc, :],
                       xnf[:, dc, 512 * g:512 * g + 512], dc == 0,
                       dc == NCH - 1, [bwt_b, xnf_b], [pb_b])
                act(k, hf[:, ht, 512 * g:512 * g + 512], pb[:, :], AF.Gelu,
                    [pb_b, c.f_b1T_b], [hf_b], bias=c.f_b1T[:, ht:ht + 1])
        c.ar.release("l1f_xnf")
        bw2b = [c.ar.alloc(f"l1f_bw2c{i}", [128, D], BF16) for i in range(4)]
        tt0 = 0
        while tt0 < tiles_per_half:
            gsz = min(3, tiles_per_half - tt0)
            for hc in range(32):
                b2c, b2c_b = bw2b[hc % 4]
                c.k.op("sp", lambda e, o=b2c, i=bw2_dram[hc]:
                       e.dma_start(out=o, in_=i), reads=(), writes=[b2c_b],
                       dma=True, deps=tuple(c.f_bw2_ticks), dur=2200)
                for i in range(gsz):
                    for dg in range(2):
                        pb, pb_b = c.psum[i * 2 + dg]
                        mm(k, pb[:, :], hf[:, hc, 128 * (tt0 + i):128 * (tt0 + i) + 128],
                           b2c[:, 512 * dg:512 * dg + 512], hc == 0, hc == 31,
                           [b2c_b, hf_b], [pb_b])
            for i in range(gsz):
                for dg in range(2):
                    pb, pb_b = c.psum[i * 2 + dg]
                    stt(k, z[:, t0 + tt0 + i, 512 * dg:512 * dg + 512],
                        pb[:, :], 1.0, c.f_b2B[:, 512 * dg:512 * dg + 512],
                        ALU.mult, ALU.add, [pb_b, c.f_b2B_b], [z_b])
            tt0 += gsz
        for i in range(4):
            c.ar.release(f"l1f_bw2c{i}")
        c.ar.release("l1f_h")
    for i in range(4):
        c.ar.release(f"l1f_bwt{i}")
    release_ln_stats(c, stats)
    emit_post_ln_residual(c, xv, xbufs, z, z_b, nt, LN_EPS, "l1f2")
    c.ar.release("l1f_z")

def build_l1f(T=2048, nseq=2):
    nt = T // 128
    nc = bass.Bass("TRN2", target_bir_lowering=False)
    xin = nc.dram_tensor("xin", [nseq, T, D], F32, kind="ExternalInput").ap()
    xout = nc.dram_tensor("xout", [nseq, T, D], F32, kind="ExternalOutput").ap()
    w = {"w1": nc.dram_tensor("w1", [4096, D], F32, kind="ExternalInput").ap(),
         "b1": nc.dram_tensor("b1", [4096], F32, kind="ExternalInput").ap(),
         "w2": nc.dram_tensor("w2", [D, 4096], F32, kind="ExternalInput").ap(),
         "b2": nc.dram_tensor("b2", [D], F32, kind="ExternalInput").ap()}
    bw1_dram = nc.dram_tensor("bw1T", [NCH, 128, 4096], BF16, kind="Internal").ap()
    bw2_dram = nc.dram_tensor("bw2T", [32, 128, D], BF16, kind="Internal").ap()
    with ExitStack() as stack:
        c = Ctx()
        c.nc = nc
        c.k = K(nc)
        c.ar = Arena(nc, stack, "arena", 52000)
        c.psum, c.psumT, c.psumT2 = make_psum(nc, stack)
        sems = {s: stack.enter_context(nc.semaphore(s)) for s in c.k.semnames}
        emit_constants(c)
        emit_l1f_weights(c, w)
        _binarize_w2(c, w, bw2_dram)
        for b in range(nseq):
            xv, x_buf = c.ar.alloc("x", [128, nt, D], F32)
            xbufs = [x_buf] + [Buf(f"x{t}") for t in range(1, nt)]
            dma(c.k, "pool", xv, xin[b].rearrange("(t p) d -> p t d", p=128),
                (), xbufs)
            emit_l1f_seq(c, xv, xbufs, nt, w, bw1_dram, bw2_dram, b == 0)
            dma(c.k, "pool", xout[b].rearrange("(t p) d -> p t d", p=128), xv,
                xbufs, ())
            c.ar.release("x", xbufs)
        c.k.barrier("pool", [])
        block = stack.enter_context(nc.Block())
        c.k.playback(block, sems)
    return nc

def release_l1f_weights(c):
    for nm in ("l1f_b1T", "l1f_b2B"):
        c.ar.release(nm)

L0A_W = ("W_dq", "W_uq", "W_dkv", "W_uk", "W_uv", "W_o")
L0F_W = ("ln_gamma", "ln_beta", "mu", "scale", "w1", "b1", "w2", "b2")
L1A_W = ("Wq", "bq", "Wk", "bk", "Wv", "bv", "Wo", "bo", "Wg", "bg", "Wc",
         "bc", "Ws", "bs")
L1F_W = ("w1", "b1", "w2", "b2")
_SHAPES = {
    "L0_attn_W_dq": (QL, D), "L0_attn_W_uq": (D, QL), "L0_attn_W_dkv": (KVL, D),
    "L0_attn_W_uk": (D, KVL), "L0_attn_W_uv": (D, KVL), "L0_attn_W_o": (D, D),
    "L0_ffn_ln_gamma": (D,), "L0_ffn_ln_beta": (D,), "L0_ffn_mu": (D, 16, 16),
    "L0_ffn_scale": (1,), "L0_ffn_w1": (16, D, 256), "L0_ffn_b1": (16, 256),
    "L0_ffn_w2": (16, 256, D), "L0_ffn_b2": (16, D),
    "L1_attn_Wq": (D, D), "L1_attn_bq": (D,), "L1_attn_Wk": (D, D),
    "L1_attn_bk": (D,), "L1_attn_Wv": (D, D), "L1_attn_bv": (D,),
    "L1_attn_Wo": (3 * D, D), "L1_attn_bo": (D,), "L1_attn_Wg": (D, 3),
    "L1_attn_bg": (3,), "L1_attn_Wc": (4 * D, D), "L1_attn_bc": (D,),
    "L1_attn_Ws": (D, 1), "L1_attn_bs": (1,),
    "L1_ffn_w1": (4 * D, D), "L1_ffn_b1": (4 * D,), "L1_ffn_w2": (D, 4 * D),
    "L1_ffn_b2": (D,),
}

def build_fused(T=2048, nseq=2):
    nt = T // 128
    nc = bass.Bass("TRN2", target_bir_lowering=False)
    xin = nc.dram_tensor("xin", [nseq, T, D], F32, kind="ExternalInput").ap()
    xout = nc.dram_tensor("xout", [nseq, T, D], F32, kind="ExternalOutput").ap()
    wd = {n: nc.dram_tensor(n, list(s), F32, kind="ExternalInput").ap()
          for n, s in _SHAPES.items()}
    wl0a = {n: wd["L0_attn_" + n] for n in L0A_W}
    wl0f = {n: wd["L0_ffn_" + n] for n in L0F_W}
    wl1a = {n: wd["L1_attn_" + n] for n in L1A_W}
    wl1f = {n: wd["L1_ffn_" + n] for n in L1F_W}
    bw1_dram = nc.dram_tensor("bw1T", [NCH, 128, 4096], BF16, kind="Internal").ap()
    bw2_dram = nc.dram_tensor("bw2T", [32, 128, D], BF16, kind="Internal").ap()
    with ExitStack() as stack:
        c = Ctx()
        c.nc = nc
        c.k = K(nc)
        c.ar = Arena(nc, stack, "arena", 52000)
        c.psum, c.psumT, c.psumT2 = make_psum(nc, stack)
        sems = {s: stack.enter_context(nc.semaphore(s)) for s in c.k.semnames}
        c.dbg_win = None
        emit_constants(c)
        for b in range(nseq):
            xv, x_buf = c.ar.alloc("x", [128, nt, D], F32)
            xbufs = [x_buf] + [Buf(f"x{t}") for t in range(1, nt)]
            dma(c.k, "pool", xv, xin[b].rearrange("(t p) d -> p t d", p=128),
                (), xbufs)
            emit_l0a_weights(c, wl0a)
            emit_l0f_weights(c, wl0f)
            emit_l0a_seq(c, xv, xbufs, nt)
            release_l0a_weights(c)
            emit_l1a_weights(c, wl1a)
            emit_l0f_seq(c, xv, xbufs, nt, wl0f)
            release_l0f_weights(c)
            emit_l1a_seq(c, xv, xbufs, nt, wl1a)
            release_l1a_weights(c)
            emit_l1f_weights(c, wl1f)
            if b == 0:
                _binarize_w2(c, wl1f, bw2_dram)
            emit_l1f_seq(c, xv, xbufs, nt, wl1f, bw1_dram, bw2_dram, b == 0)
            release_l1f_weights(c)
            dma(c.k, "pool", xout[b].rearrange("(t p) d -> p t d", p=128), xv,
                xbufs, ())
            c.ar.release("x", xbufs)
        c.k.barrier("pool", [])
        block = stack.enter_context(nc.Block())
        c.k.playback(block, sems)
    return nc

def kernel(**inputs):
    ncores = 8
    x = np.ascontiguousarray(inputs["L0_attn_x"])
    nc = build_fused()
    in_maps = []
    for cid in range(ncores):
        m = {"xin": x[2 * cid:2 * cid + 2]}
        for n in _SHAPES:
            m[n] = inputs[n]
        in_maps.append(m)
    res = run_bass_kernel_spmd(nc, in_maps, core_ids=list(range(ncores)))
    return np.concatenate([r["xout"] for r in res.results], axis=0)
